# version 36
# baseline (speedup 1.0000x reference)
import numpy as np
import ml_dtypes
import concourse.bass as bass
import concourse.mybir as mybir
from concourse.alu_op_type import AluOpType as Alu
from concourse.bass_utils import run_bass_kernel_spmd
from contextlib import ExitStack

F32 = mybir.dt.float32
BF16 = mybir.dt.bfloat16
AF = mybir.ActivationFunctionType

P = 128
D = 1024; KD = 8
DI = 2048; KI = 16
NS = 16
DTR = 64
DFF = 4096; KF = 32
T = 8192
L = 512; NCH = T // L
LP4 = L + 4
EPS = 1e-5
NGRP = 4


SMV = {}
_off = 0
for _name, _n in (("inb", 64), ("convw", 128), ("convb", 32), ("dtb", 32),
                  ("outb", 16), ("rmsw", 16), ("gam", 8), ("bet", 8),
                  ("fb1", 32), ("fb2", 8), ("scb", 8), ("fbias", 8),
                  ("scbs", 8), ("cst", 8)):
    SMV[_name] = _off
    _off += _n
assert _off <= 384


class Bld:
    SECTIONS = ("sync", "scalar", "vector", "tensor", "gpsimd")

    def __init__(self, passes=(1, 1, 1), debug=False):
        self.passes = passes
        self.debug = debug
        self.ops = {s: [] for s in self.SECTIONS}
        self.cnt = {}
        self.rt = {}
        self.wt = {}
        self.bk_w = [None] * 8
        self.bk_r = [None] * 8
        self.store_tok = {}
        self.qidx = {}
        self.rms_tok = {}

    def _emit(self, section, fn, waits=(), inc=None):
        w = tuple(t for t in waits if t is not None)
        tok = None
        if inc is not None:
            name, amt = inc
            self.cnt[name] = self.cnt.get(name, 0) + amt
            tok = (name, self.cnt[name], amt)
        self.ops[section].append((fn, w, tok))
        return tok

    def V(self, fn, waits=(), inc=True):
        return self._emit("vector", fn, waits, ("v", 1) if inc else None)

    def S(self, fn, waits=(), inc=True):
        return self._emit("scalar", fn, waits, ("s", 1) if inc else None)

    def Tm(self, fn, waits=(), inc=False):
        return self._emit("tensor", fn, waits, ("t", 1) if inc else None)

    NSEM = 4

    def _dma(self, section, qname, out, in_, waits):
        i = self.qidx.get(qname, 0)
        self.qidx[qname] = i + 1
        sem = f"{qname}{i % self.NSEM}"
        pred = self.cnt.get(sem, 0)
        w = list(waits)
        if pred:
            w.append((sem, pred, 0))
        return self._emit(section, lambda e: e.dma_start(out=out, in_=in_),
                          w, (sem, 16))

    def Y(self, out, in_, waits=()):
        return self._dma("sync", "y", out, in_, waits)

    def A(self, out, in_, waits=()):
        return self._dma("scalar", "a", out, in_, waits)

    def G(self, out, in_, waits=()):
        return self._dma("gpsimd", "g", out, in_, waits)

    def rd(self, key, tok):
        if tok is None:
            return tok
        d = self.rt.setdefault(key, {})
        cur = d.get(tok[0])
        if cur is None or cur[1] < tok[1]:
            d[tok[0]] = tok
        return tok

    def war(self, key):
        return list(self.rt.get(key, {}).values())

    def wrote(self, key, tok):
        self.rt[key] = {}
        self.wt[key] = tok
        return tok

    def col(self, tensor, c):
        return tensor[:, c:c + 1]

    def cnt_toks_q(self, q):
        toks = []
        for i in range(self.NSEM):
            n = self.cnt.get(f"{q}{i}", 0)
            if n:
                toks.append((f"{q}{i}", n, 0))
        return toks

    OPH_BANKS = (0, 1, 4, 5)

    def emit_vy(self):
        if getattr(self, "vy_pend", None) is None:
            return None
        k, ybank, ymm = self.vy_pend
        self.vy_pend = None
        t, ps = self.t, self.t.ps
        ysl = t.scrA[:, (k % KD) * L:(k % KD + 1) * L]
        sev = self.S(lambda e, ysl=ysl, ybank=ybank: e.activation(
            out=ysl, in_=ps[ybank][:], func=AF.Identity),
            waits=[ymm] + self.war("scrA"))
        self.wrote("scrA", sev)
        self.bk_r[ybank] = sev
        vy = self.V(lambda e, k=k, ysl=ysl: e.tensor_tensor(
            t.ybf[:, k * L:(k + 1) * L], ysl,
            t.ybf[:, k * L:(k + 1) * L], Alu.mult),
            waits=[sev, self.sz_tok[k]])
        self.rd("scrA", vy)
        return vy

    @staticmethod
    def pick(tok, k):
        if (isinstance(tok, tuple) and len(tok) == 2
                and isinstance(tok[0], tuple)):
            return tok[k // 4]
        return tok

    def build(self):
        nc = bass.Bass("TRN2", target_bir_lowering=False,
                       detect_race_conditions=False)
        self.nc = nc
        dt = nc.dram_tensor

        x_t = dt("x_t", [D, T], F32, kind="ExternalInput").ap()
        win = {}
        for br in ("m1", "m2"):
            win[br, "inw"] = dt(f"{br}_inw", [P, KD * 4096], F32, kind="ExternalInput").ap()
            win[br, "outw"] = dt(f"{br}_outw", [P, KI * 1024], F32, kind="ExternalInput").ap()
            win[br, "xprj"] = dt(f"{br}_xprj", [P, KI * 128], F32, kind="ExternalInput").ap()
            win[br, "dtw"] = dt(f"{br}_dtw", [DTR, DI], F32, kind="ExternalInput").ap()
            win[br, "dD"] = dt(f"{br}_dD", [P, DI], F32, kind="ExternalInput").ap()
        alog_in = dt("alog", [P, 512], F32, kind="ExternalInput").ap()
        smalls = dt("smalls", [P, 384], F32, kind="ExternalInput").ap()
        fw1 = dt("fw1", [P, KD * 4096], F32, kind="ExternalInput").ap()
        fw2 = dt("fw2", [P, KF * 1024], F32, kind="ExternalInput").ap()
        cbt = dt("cbt", [P, KD * 1024], F32, kind="ExternalInput").ap()
        ident_in = dt("ident", [P, P], BF16, kind="ExternalInput").ap()
        identf_in = dt("identf", [P, P], F32, kind="ExternalInput").ap()
        sel_in = dt("sel", [48, NS * P], BF16, kind="ExternalInput").ap()
        onec_in = dt("onec", [P, 1], BF16, kind="ExternalInput").ap()
        oner_in = dt("oner", [1, P], F32, kind="ExternalInput").ap()

        out_t = dt("out_t", [D, T], F32, kind="ExternalOutput").ap()
        wbf = dt("wbf", [P, 4 * 32768 + 16384], BF16, kind="Internal").ap()
        skind = "ExternalOutput" if self.debug else "Internal"
        if self.debug:
            self.t_dbg1 = dt("dbg1", [DI, T], BF16, kind="ExternalOutput").ap()
            self.t_dbg2 = dt("dbg2", [DI, T], BF16, kind="ExternalOutput").ap()
        h_s = dt("h_s", [D, T], F32, kind=skind).ap()
        h2_s = dt("h2_s", [D, T], F32, kind=skind).ap()
        u_s = dt("u_s", [D, T], F32, kind=skind).ap()
        hh_s = dt("hh_s", [D, T], F32, kind=skind).ap()
        rms_d = dt("rms_d", [1, T], F32, kind="Internal").ap()

        with ExitStack() as ctx:
            sb = lambda n, s, d: ctx.enter_context(nc.sbuf_tensor(n, s, d))
            fA = sb("fA", [P, KD * L], F32)
            fC = sb("fC", [P, KD * L], F32)
            xcu = sb("xcu", [P, KI * LP4], BF16)
            brep = sb("brep", [P, NS * L], BF16)
            crep = sb("crep", [P, NS * L], BF16)
            ybf = sb("ybf", [P, KI * L], BF16)
            scrA = sb("scrA", [P, KD * L], BF16)
            W1 = sb("W1", [P, KI * 1024], BF16)
            R = sb("R", [P, 2 * 2048], BF16)
            dtw = sb("dtw", [P, DI], BF16)
            xprj = sb("xprj", [P, KI * P], BF16)
            dD = sb("dD", [P, DI], BF16)
            A_all = sb("A_all", [P, 256], F32)
            smv = sb("smv", [P, 384], F32)
            ident = sb("ident_sb", [P, P], BF16)
            identf = sb("identf_sb", [P, P], F32)
            sel = sb("sel_sb", [48, NS * P], BF16)
            onec = sb("onec_sb", [P, 1], BF16)
            oner = sb("oner_sb", [1, P], F32)
            rows4 = sb("rows4", [1, L], F32)
            cring = sb("cring", [P, 2 * L], BF16)
            sqring = sb("sqring", [P, 2 * L], BF16)
            rb = sb("rb", [P, L], F32)
            mrb = sb("mrb", [P, L], F32)
            spr = sb("spr", [P, L], F32)
            dlt = sb("dlt", [P, L], BF16)
            BC = sb("BC", [48, L], BF16)
            dring = sb("dring", [P, 2 * L], BF16)
            during = sb("during", [P, L], BF16)
            ering = sb("ering", [P, 2 * L], F32)
            dbub = sb("dbub", [P, 2 * NGRP * L], BF16)
            hring = sb("hring", [P, NGRP * L], BF16)
            state = sb("state", [P, 256], BF16)
            hist0 = sb("hist0", [P, KI * 3], BF16)
            hist1 = sb("hist1", [P, KI * 3], BF16)
            tmpr = sb("tmpr", [P, L], F32)
            uor = sb("uor", [P, 2 * L], F32)
            ldr = sb("ldr", [P, 2 * L], F32)
            ps = [ctx.enter_context(nc.psum_tensor(f"ps{i}", [P, L], F32))
                  for i in range(8)]
            semnames = ["v", "s", "t"] + [f"{q}{i}" for q in ("y", "a", "g")
                                          for i in range(self.NSEM)]
            sems = {n: ctx.enter_context(nc.semaphore(f"sem_{n}"))
                    for n in semnames}

            st = dict(locals())
            self.t = type("NS_", (), st)
            self.t.x_t = x_t; self.t.win = win; self.t.wbf = wbf
            self.t.out_t = out_t
            self.t.h_s = h_s; self.t.h2_s = h2_s
            self.t.u_s = u_s; self.t.hh_s = hh_s
            self.t.rms_d = rms_d
            self.t.alog_in = alog_in; self.t.smalls = smalls
            self.t.fw1 = fw1; self.t.fw2 = fw2; self.t.cbt = cbt
            self.t.ident_in = ident_in; self.t.sel_in = sel_in
            self.t.onec_in = onec_in; self.t.oner_in = oner_in
            self.t.identf_in = identf_in

            block = ctx.enter_context(nc.Block())

            self.emit_prologue()
            if self.passes[0]:
                self.emit_branch_pass("m1", 1)
            if self.passes[1]:
                self.emit_branch_pass("m2", 2)
            self.emit_epilogue()

            for sec in self.SECTIONS:
                own = {"vector": "v", "scalar": "s", "tensor": "t"}.get(sec)
                oplist = self.ops[sec]

                def make(oplist=oplist, own=own):
                    def body(eng):
                        waited = {}
                        for fn, waits, tok in oplist:
                            for wt in waits:
                                sem, val = wt[0], wt[1]
                                if sem == own:
                                    continue
                                if waited.get(sem, -1) >= val:
                                    continue
                                eng.wait_ge(sems[sem], val)
                                waited[sem] = val
                            inst = fn(eng)
                            if tok is not None:
                                inst.then_inc(sems[tok[0]], tok[2])
                    return body
                getattr(block, sec)(make())
        return nc

    def emit_prologue(self):
        t = self.t
        y1 = self.Y(t.smv[:], t.smalls)
        y2 = y1
        y3 = self.Y(t.oner[:], t.oner_in)
        self.tok_consts = y3
        self.tok_identf = self.Y(t.identf[:], t.identf_in)
        self.ld_x0 = self.Y(chunk_dst(t.fA), chunk_ap(t.x_t, 0))
        self.G(t.ident[:], t.ident_in)
        self.G(t.sel[:], t.sel_in)
        self.tok_constg = self.G(t.onec[:], t.onec_in)
        srcs = [t.win["m1", "inw"], t.win["m2", "inw"], t.fw1, t.fw2,
                t.win["m2", "outw"]]
        self.precast_piece = {}
        for si, srct in enumerate(srcs):
            for piece in range(4 if si < 4 else 2):
                half = (si * 4 + piece) % 2
                stg = t.W1[:, half * 8192:(half + 1) * 8192]
                gld = self.G(stg, srct[:, piece * 8192:(piece + 1) * 8192],
                             waits=self.war(("W1h", half)))
                self.wrote(("W1h", half), gld)
                yst = self.Y(t.wbf[:, si * 32768 + piece * 8192:
                                   si * 32768 + (piece + 1) * 8192],
                             stg, waits=[gld])
                self.rd(("W1h", half), yst)
                self.rd("W1", yst)
                self.precast_piece[si * 4 + piece] = yst
        self.precast_toks = self.cnt_toks_q("y")
        sc = SMV
        self.tok_scb = self.V(lambda e: e.tensor_scalar(
            out=t.smv[:, sc["scbs"]:sc["scbs"] + 8],
            in0=t.smv[:, sc["scb"]:sc["scb"] + 8],
            scalar1=1.0 / 127.0, scalar2=None, op0=Alu.mult),
            waits=[y1, y2, y3])

    def ln_stats(self, src, src_tok, with_mean, src_key=None,
                 combined=False, save_row=None):
        t, ps = self.t, self.t.ps
        onef = t.smv[:, SMV["cst"] + 1:SMV["cst"] + 2]
        mm_tok = None
        mm1_tok = None
        rd0 = self.bk_r[0]
        rd1 = self.bk_r[1]
        for k in range(KD):
            sl = slice(k * L, (k + 1) * L)
            cs = slice((k % 2) * L, (k % 2 + 1) * L)
            stk = self.pick(src_tok, k)
            stok = self.S(lambda e, sl=sl, cs=cs: e.activation(
                out=t.sqring[:, cs], in_=src[:, sl], func=AF.Square),
                waits=[stk] + self.war(("sqring", k % 2)))
            self.wrote(("sqring", k % 2), stok)
            if src_key:
                self.rd(src_key, stok)
            if with_mean:
                w0 = ([stk, rd0, self.tok_consts] if k == 0 else [stk])
                mm1 = self.Tm(lambda e, sl=sl, k=k, onef=onef: e.matmul(
                    ps[0][0:1, :], onef, src[:, sl],
                    start=(k == 0), stop=(k == KD - 1)),
                    waits=w0, inc=True)
                if src_key:
                    self.rd(src_key, mm1)
                mm1_tok = mm1
            w0 = ([stok, rd1, self.tok_constg] if k == 0 else [stok])
            mm2 = self.Tm(lambda e, cs=cs, k=k: e.matmul(
                ps[1][0:1, :], t.onec[:], t.sqring[:, cs],
                start=(k == 0), stop=(k == KD - 1)),
                waits=w0, inc=True)
            self.rd(("sqring", k % 2), mm2)
            mm_tok = mm2
        self.bk_w[0] = mm1_tok
        self.bk_w[1] = mm_tok
        sl0 = t.spr[0:1, 0:L]
        sl1 = t.mrb[0:1, 0:L]
        scx = t.rb[0:1, 0:L]
        war_sr = self.war(("sp", 0))
        war_rbrow = self.war("rb")
        war_mrow = self.war("mrb")
        if with_mean:
            vmu = self.V(lambda e: e.tensor_scalar(
                out=sl0, in0=ps[0][0:1, :], scalar1=1.0 / D,
                scalar2=None, op0=Alu.mult), waits=[mm1_tok] + war_sr)
            self.rd(("sp", 0), vmu)
            self.bk_r[0] = vmu
        vms = self.V(lambda e: e.tensor_scalar(
            out=sl1, in0=ps[1][0:1, :], scalar1=1.0 / D,
            scalar2=None, op0=Alu.mult), waits=[mm_tok] + war_mrow)
        self.bk_r[1] = vms
        if with_mean:
            self.V(lambda e: e.tensor_tensor(scx, sl0, sl0, Alu.mult),
                   waits=war_rbrow)
            vvar = self.V(lambda e: e.tensor_tensor(sl1, sl1, scx,
                                                    Alu.subtract))
        else:
            vvar = vms
        sln = self.S(lambda e: e.activation(
            out=scx, in_=sl1, func=AF.Ln,
            bias=t.smv[0:1, SMV["cst"]:SMV["cst"] + 1]),
            waits=[vvar, self.tok_consts] + war_rbrow)
        sln2 = None
        if combined:
            sln2 = self.S(lambda e: e.activation(
                out=t.rows4[0:1, 0:L], in_=sl1, func=AF.Ln,
                scale=t.smv[0:1, SMV["cst"] + 4:SMV["cst"] + 5],
                bias=t.smv[0:1, SMV["cst"] + 5:SMV["cst"] + 6]),
                waits=self.war("rows4"))
        s2 = self.S(lambda e: e.activation(
            out=sl1, in_=scx, func=AF.Exp,
            scale=t.smv[0:1, SMV["cst"] + 2:SMV["cst"] + 3]))
        if save_row is not None:
            srt = self.A(t.rms_d[0:1, save_row * L:(save_row + 1) * L],
                         sl1, waits=[s2])
            self.rd("mrb", srt)
            self.rms_tok[save_row] = srt
        s3 = None
        if combined:
            vdiff = self.V(lambda e: e.tensor_tensor(
                t.rows4[0:1, 0:L], t.rows4[0:1, 0:L], scx,
                Alu.subtract), waits=[sln2, sln])
            self.wrote("rows4", vdiff)
            s3 = self.S(lambda e: e.activation(
                out=t.rows4[0:1, 0:L], in_=t.rows4[0:1, 0:L],
                func=AF.Exp,
                scale=t.smv[0:1, SMV["cst"] + 2:SMV["cst"] + 3]),
                waits=[vdiff])
            self.rd("rows4", s3)
        tok_mr = None
        if with_mean:
            tok_mr = self.V(lambda e: e.tensor_tensor(sl0, sl0, sl1,
                                                      Alu.mult), waits=[s2])
            self.rd(("sp", 0), tok_mr)
        b1 = self.Tm(lambda e: e.matmul(ps[0][:], t.oner[0:1, :],
                                        sl1, start=True, stop=True),
                     waits=[s2, self.bk_r[0], self.tok_consts], inc=True)
        self.rd("mrb", b1)
        self.bk_w[0] = b1
        vrb = self.S(lambda e: e.activation(
            out=t.rb[:], in_=ps[0][:], func=AF.Identity),
            waits=[b1, s2])
        self.wrote("rb", vrb)
        self.bk_r[0] = vrb
        vmrb = None
        if with_mean:
            b2 = self.Tm(lambda e: e.matmul(ps[1][:], t.oner[0:1, :],
                                            sl0, start=True, stop=True),
                         waits=[tok_mr, self.bk_r[1]], inc=True)
            self.rd(("sp", 0), b2)
            self.bk_w[1] = b2
            vmrb = self.S(lambda e: e.activation(
                out=t.mrb[:], in_=ps[1][:], func=AF.Identity),
                waits=[b2] + self.war("mrb"))
            self.wrote("mrb", vmrb)
            self.bk_r[1] = vmrb
        if combined:
            b3 = self.Tm(lambda e: e.matmul(ps[0][:], t.oner[0:1, :],
                                            t.rows4[0:1, 0:L],
                                            start=True, stop=True),
                         waits=[s3, vrb], inc=True)
            self.rd("rows4", b3)
            self.bk_w[0] = b3
            return vrb, vmrb, b3
        return vrb, vmrb

    def pair_load(self, wsrc, pr, slot):
        return self.pair_load_off(self.wbase + pr * 2048, slot)

    def pair_load_off(self, off, slot):
        t = self.t
        dst = t.R[:, slot * 2048:(slot + 1) * 2048]
        tok = self.Y(dst, t.wbf[:, off:off + 2048],
                     waits=self.war(("R", slot)) +
                     [self.precast_piece[off // 8192]])
        self.wrote(("R", slot), tok)
        return tok

    def riter_merged(self):
        for pr in range(16):
            yield 32768 + pr * 2048
        for c in range(NCH):
            for m in range(KD):
                yield 131072 + m * 2048
            for pr in range(16):
                yield 65536 + pr * 2048
            if c + 1 < NCH:
                for pr in range(16):
                    yield 32768 + pr * 2048

    def rnext(self, slot):
        off = next(self.riter, None)
        if off is None:
            return self.rq[slot]
        self.rq[slot] = self.pair_load_off(off, slot)
        return self.rq[slot]

    def emit_branch_pass(self, br, pidx):
        t = self.t
        self.merged = (pidx == 2)
        if pidx == 1:
            self.tok_w1load = self.wrote(
                "W1", self.G(t.W1[:], t.win[br, "outw"],
                             waits=self.war("W1")))
        else:
            w1war = self.war("W1")
            self.tok_cbload = self.wrote(
                "W1", self.G(t.W1[:, 0:KD * 1024], t.cbt, waits=w1war))
            self.w2ld = [None, None]
            self.w2ld[0] = self.A(
                t.W1[:, KD * 1024:KD * 1024 + 4096],
                t.wbf[:, 3 * 32768:3 * 32768 + 4096],
                waits=w1war + self.war(("W2", 0)) +
                [self.precast_piece[12]])
            self.wrote(("W2", 0), self.w2ld[0])
            self.w2_use = [None, None]
        self.tok_dtwload = self.wrote(
            "dtw", self.G(t.dtw[64:128, :], t.win[br, "dtw"],
                          waits=self.war("dtw")))
        self.tok_xprjload = self.wrote(
            "xprj", self.G(t.xprj[:], t.win[br, "xprj"],
                           waits=self.war("xprj")))
        self.tok_dDload = self.wrote(
            "dD", self.G(t.dD[:], t.win[br, "dD"], waits=self.war("dD")))
        boff = 0 if br == "m1" else 1
        self.wbase = boff * 32768
        ya = self.A(t.spr[:, 0:256], t.alog_in[:, boff * 256:(boff + 1) * 256],
                    waits=self.war(("sp", 0)) + [self.wt.get(("sp", 0))])
        sa_ = self.S(lambda e: e.activation(out=t.A_all[:], in_=t.spr[:, 0:256],
                                            func=AF.Exp),
                     waits=[ya] + self.war("A_all"))
        self.rd(("sp", 0), sa_)
        self.tok_A = self.wrote("A_all", self.V(
            lambda e: e.tensor_scalar(out=t.A_all[:], in0=t.A_all[:],
                                      scalar1=-1.0, scalar2=None,
                                      op0=Alu.mult), waits=[sa_]))
        self.tok_statez = self.wrote(
            "state", self.V(lambda e: e.memset(t.state[:], 0.0),
                            waits=self.war("state")))
        vz = self.V(lambda e: e.memset(t.hist0[:], 0.0),
                    waits=self.war("hist0"))
        self.wrote("hist0", vz)
        vz2 = self.V(lambda e: e.memset(t.hist1[:], 0.0),
                     waits=self.war("hist1"))
        self.wrote("hist1", vz2)
        self.tok_histz = vz2
        if pidx == 1:
            self.ld_x = self.ld_x0
            self.ld_h2 = None
        else:
            self.ld_x = self.Y(chunk_dst(t.fA), chunk_ap(t.u_s, 0),
                               waits=self.war("fA") +
                               [self.store_tok.get(("u", 0))])
            self.ld_h2 = self.A(chunk_dst(t.fC), chunk_ap(t.h2_s, 0),
                                waits=self.war("fC") +
                                [self.store_tok.get(("h2", 0))])
        self.wrote("fA", self.ld_x)
        if self.ld_h2:
            self.wrote("fC", self.ld_h2)
        self.rq = [None, None]
        if self.merged:
            self.riter = self.riter_merged()
            for sl in (0, 1):
                self.rnext(sl)
        else:
            for pr in (0, 1):
                self.rq[pr % 2] = self.pair_load(t.win[br, "inw"], pr, pr % 2)
        self.prestate = {}
        self.pre_gen = self.gen_pre(br, pidx, 0)
        self.drain_pre(None)
        for c in range(NCH):
            self.emit_chunk_main(br, pidx, c)

    def drain_pre(self, n=None):
        if self.pre_gen is None:
            return
        try:
            if n is None:
                while True:
                    next(self.pre_gen)
            else:
                for _ in range(n):
                    next(self.pre_gen)
        except StopIteration:
            self.pre_gen = None

    def gen_pre(self, br, pidx, c):
        t, ps = self.t, self.t.ps
        sc = SMV
        boff = 0 if br == "m1" else 1
        ld_x = self.ld_x
        ld_h2 = self.ld_h2

        if pidx == 1:
            vrb, vmrb, b3 = self.ln_stats(t.fA, ld_x, True, "fA",
                                          combined=True)
            yield
            tok_h = None
            for k in range(KD):
                sl = slice(k * L, (k + 1) * L)
                self.V(lambda e, sl=sl: e.tensor_tensor(
                    t.fA[:, sl], t.fA[:, sl], t.rb[:], Alu.mult),
                    waits=[vrb, ld_x])
                tok_h = self.V(lambda e, sl=sl: e.tensor_tensor(
                    t.fA[:, sl], t.fA[:, sl], t.mrb[:], Alu.subtract),
                    waits=[vmrb])
                if k == 3:
                    yield
            self.rd("rb", tok_h)
            self.rd("mrb", tok_h)
            self.wrote("fA", tok_h)
            st_h = self.A(chunk_ap(t.h_s, c), chunk_dst(t.fA),
                          waits=[tok_h])
            self.store_tok[("h", c)] = self.rd("fA", st_h)
            vrr = self.S(lambda e: e.activation(
                out=t.rb[:], in_=ps[0][:], func=AF.Identity),
                waits=[b3] + self.war("rb"))
            self.wrote("rb", vrr)
            self.bk_r[0] = vrr
            yield
            tok_h2 = None
            for k in range(KD):
                sl = slice(k * L, (k + 1) * L)
                sps = slice(0, L)
                v1 = self.V(lambda e, sl=sl, sps=sps: e.tensor_tensor(
                    t.spr[:, sps], t.fA[:, sl], t.rb[:], Alu.mult),
                    waits=[vrr] + self.war(("sp", 0)))
                self.rd("fA", v1)
                self.rd("rb", v1)
                self.wrote(("sp", 0), v1)
                tok_h2 = self.S(lambda e, sl=sl, sps=sps, k=k: e.activation(
                    out=t.fC[:, sl], in_=t.spr[:, sps], func=AF.Identity,
                    scale=self.col(t.smv, sc["gam"] + k),
                    bias=self.col(t.smv, sc["bet"] + k)),
                    waits=[v1] + self.war("fC"))
                self.rd(("sp", 0), tok_h2)
                if k == 3:
                    yield
            self.wrote("fC", tok_h2)
            st2 = self.A(chunk_ap(t.h2_s, c), chunk_dst(t.fC),
                         waits=[tok_h2])
            self.store_tok[("h2", c)] = self.rd("fC", st2)
            if c + 1 < NCH:
                self.ld_x = self.Y(chunk_dst(t.fA), chunk_ap(t.x_t, c + 1),
                                   waits=self.war("fA"))
                self.wrote("fA", self.ld_x)
            yield
        else:
            tok_h2 = ld_h2
            for _ in range(4):
                yield

        if pidx == 1:
            vrq, _ = self.ln_stats(t.fC, tok_h2, False, "fC", save_row=c)
        else:
            ldrow = self.G(t.rows4[0:1, 0:L],
                           t.rms_d[0:1, c * L:(c + 1) * L],
                           waits=self.war("rows4") + [self.rms_tok[c]])
            self.wrote("rows4", ldrow)
            b1 = self.Tm(lambda e: e.matmul(
                ps[0][:], t.oner[0:1, :],
                t.rows4[0:1, 0:L], start=True, stop=True),
                waits=[ldrow, self.bk_r[0], self.tok_consts],
                inc=True)
            self.rd("rows4", b1)
            self.bk_w[0] = b1
            vrq = self.S(lambda e: e.activation(
                out=t.rb[:], in_=ps[0][:], func=AF.Identity),
                waits=[b1] + self.war("rb"))
            self.wrote("rb", vrq)
            self.bk_r[0] = vrq
        yield
        tok_xr = None
        for k in range(KD):
            sl = slice(k * L, (k + 1) * L)
            th2 = self.pick(tok_h2, k)
            w = [vrq, th2] + (self.war("scrA") if k == 0 else [])
            tok_xr = self.V(lambda e, sl=sl, k=k: e.scalar_tensor_tensor(
                t.scrA[:, sl], t.fC[:, sl],
                self.col(t.smv, sc["rmsw"] + boff * 8 + k), t.rb[:],
                Alu.mult, Alu.mult), waits=w)
            self.rd("fC", tok_xr)
            self.rd("rb", tok_xr)
        self.wrote("scrA", tok_xr)
        yield

        hp = t.hist0 if c % 2 == 0 else t.hist1
        hn = t.hist1 if c % 2 == 0 else t.hist0
        xcu3 = t.xcu[:].rearrange("p (k l) -> p k l", k=KI)
        hist3p = hp[:].rearrange("p (k h) -> p k h", k=KI)
        hist3n = hn[:].rearrange("p (k h) -> p k h", k=KI)
        conv_tok = [None] * KI
        pend = None
        for m in range(17):
          if m < 16:
            pr = m // 2
            slot = pr % 2
            bank = 4 + (m % 2)
            last = None
            for k in range(KD):
                lhs = t.R[:, slot * 2048 + (m % 2) * 1024 + k * P:
                          slot * 2048 + (m % 2) * 1024 + (k + 1) * P]
                w0 = [self.rq[slot], tok_xr, self.bk_r[bank]] if k == 0 else []
                last = self.Tm(lambda e, lhs=lhs, k=k, bank=bank: e.matmul(
                    ps[bank][:], lhs, t.scrA[:, k * L:(k + 1) * L],
                    start=(k == 0), stop=(k == KD - 1)),
                    waits=w0, inc=(k == KD - 1))
            self.bk_w[bank] = last
            self.rd(("R", slot), last)
            self.rd("scrA", last)
            if m % 2 == 1:
                if self.merged:
                    self.rnext(slot)
                elif pr + 2 <= 15:
                    self.rq[slot] = self.pair_load(t.win[br, "inw"], pr + 2,
                                                   slot)
          if pend is None:
            pend = (m, bank, last)
            yield
            continue
          m_cur, bank_cur, last_cur = m, bank, last
          m, bank, last = pend
          pend = (m_cur, bank_cur, last_cur) if m_cur < 16 else None
          if True:
            dst = t.xcu[:, m * LP4 + 4:m * LP4 + 4 + L]
            ev = self.S(lambda e, dst=dst, bank=bank, m=m: e.activation(
                out=dst, in_=ps[bank][:], func=AF.Identity,
                bias=self.col(t.smv, sc["inb"] + boff * 32 + m)),
                waits=[last] + self.war(("xcu", m)))
            self.bk_r[bank] = ev
            k = m
            base = k * LP4
            self.V(lambda e, k=k: e.tensor_copy(
                hist3n[:, k:k + 1, :], xcu3[:, k:k + 1, LP4 - 3:LP4]),
                waits=[ev, self.tok_histz])
            vpad = self.V(lambda e, k=k: e.tensor_copy(
                xcu3[:, k:k + 1, 1:4], hist3p[:, k:k + 1, :]))
            a1 = t.cring[:, (k % 2) * L:(k % 2 + 1) * L]
            a2 = t.sqring[:, (k % 2) * L:(k % 2 + 1) * L]
            wv = self.war(("cring", k % 2)) + self.war(("sqring", k % 2))
            v0 = self.V(lambda e, base=base, k=k, a1=a1: e.tensor_scalar(
                out=a1, in0=t.xcu[:, base + 1:base + 1 + L],
                scalar1=self.col(t.smv, sc["convw"] + boff * 64 + k * 4),
                scalar2=None, op0=Alu.mult), waits=wv)
            self.wrote(("cring", k % 2), v0)
            v1 = self.V(lambda e, base=base, k=k, a1=a1, a2=a2:
                        e.scalar_tensor_tensor(
                a2, t.xcu[:, base + 2:base + 2 + L],
                self.col(t.smv, sc["convw"] + boff * 64 + k * 4 + 1),
                a1, Alu.mult, Alu.add))
            self.wrote(("sqring", k % 2), v1)
            self.V(lambda e, base=base, k=k, a1=a1, a2=a2:
                   e.scalar_tensor_tensor(
                a1, t.xcu[:, base + 3:base + 3 + L],
                self.col(t.smv, sc["convw"] + boff * 64 + k * 4 + 2),
                a2, Alu.mult, Alu.add))
            v6 = self.V(lambda e, base=base, k=k, a1=a1, a2=a2:
                        e.scalar_tensor_tensor(
                a2, t.xcu[:, base + 4:base + 4 + L],
                self.col(t.smv, sc["convw"] + boff * 64 + k * 4 + 3),
                a1, Alu.mult, Alu.add))
            self.rd(("cring", k % 2), v6)
            self.rd(("sqring", k % 2), v6)
            s1 = self.S(lambda e, base=base, k=k, a2=a2: e.activation(
                out=t.xcu[:, base + 4:base + 4 + L], in_=a2, func=AF.Silu,
                bias=self.col(t.smv, sc["convb"] + boff * 16 + k)),
                waits=[v6])
            self.rd(("cring", k % 2), s1)
            self.rd(("sqring", k % 2), s1)
            self.wrote(("xcu", m), s1)
            conv_tok[k] = s1
            if pend is not None:
                yield
        self.prestate[c] = dict(conv_tok=conv_tok, tok_xr=tok_xr)

    def emit_chunk_main(self, br, pidx, c):
        t, ps = self.t, self.t.ps
        sc = SMV
        boff = 0 if br == "m1" else 1
        ld_x = self.ld_x
        ld_h2 = self.ld_h2

        pre = self.prestate.pop(c)
        conv_tok = pre["conv_tok"]
        tok_xr = pre["tok_xr"]
        if pidx == 2 and c + 1 < NCH:
            self.ld_h2 = self.A(chunk_dst(t.fC), chunk_ap(t.h2_s, c + 1),
                                waits=self.war("fC") +
                                [self.store_tok.get(("h2", c + 1))])
            self.wrote("fC", self.ld_h2)

        sz_tok = [None] * KI
        self.sz_tok = sz_tok
        for k in range(KI):
            mz = 16 + k
            pz = mz // 2
            slot = pz % 2
            zbank = 4 + (k % 2)
            zlast = None
            for kk in range(KD):
                lhs = t.R[:, slot * 2048 + (mz % 2) * 1024 + kk * P:
                          slot * 2048 + (mz % 2) * 1024 + (kk + 1) * P]
                w0 = [self.rq[slot], tok_xr, self.bk_r[zbank]] if kk == 0 else []
                zlast = self.Tm(lambda e, lhs=lhs, kk=kk, zbank=zbank:
                                e.matmul(
                    ps[zbank][:], lhs, t.scrA[:, kk * L:(kk + 1) * L],
                    start=(kk == 0), stop=(kk == KD - 1)),
                    waits=w0, inc=(kk == KD - 1))
            self.bk_w[zbank] = zlast
            self.rd(("R", slot), zlast)
            self.rd("scrA", zlast)
            if mz % 2 == 1:
                if self.merged:
                    self.rnext(slot)
                else:
                    np_ = pz + 2
                    if np_ <= 15:
                        self.rq[slot] = self.pair_load(t.win[br, "inw"], np_,
                                                       slot)
                    elif c + 1 < NCH:
                        self.rq[slot] = self.pair_load(t.win[br, "inw"],
                                                       np_ - 16, slot)
            ev = self.S(lambda e, zbank=zbank, k=k, mz=mz: e.activation(
                out=t.ybf[:, k * L:(k + 1) * L], in_=ps[zbank][:],
                func=AF.Silu,
                bias=self.col(t.smv, sc["inb"] + boff * 32 + mz)),
                waits=[zlast] + (self.war("ybf") if k == 0 else []))
            self.bk_r[zbank] = ev
            sz_tok[k] = ev
        self.wrote("ybf", sz_tok[-1])

        last = None
        for k in range(KI):
            lhs = t.xprj[:, k * P:(k + 1) * P]
            rhs = t.xcu[:, k * LP4 + 4:k * LP4 + 4 + L]
            w0 = ([conv_tok[-1], self.tok_xprjload, self.bk_r[4]]
                  if k == 0 else [conv_tok[k]])
            last = self.Tm(lambda e, lhs=lhs, rhs=rhs, k=k: e.matmul(
                ps[4][:], lhs, rhs, start=(k == 0), stop=(k == KI - 1)),
                waits=w0, inc=(k == KI - 1))
        self.bk_w[4] = last
        self.rd("xprj", last)
        for k in range(KI):
            self.rd(("xcu", k), last)
        e1 = self.S(lambda e: e.activation(out=t.dlt[64:128, :],
                                           in_=ps[4][64:128, :],
                                           func=AF.Identity),
                    waits=[last] + self.war("dlt"))
        self.wrote("dlt", e1)
        e2 = self.S(lambda e: e.activation(out=t.BC[0:16, :],
                                           in_=ps[4][0:16, :],
                                           func=AF.Identity),
                    waits=self.war("BC"))
        e3 = self.S(lambda e: e.activation(out=t.BC[32:48, :],
                                           in_=ps[4][32:48, :],
                                           func=AF.Identity))
        self.wrote("BC", e3)
        self.bk_r[4] = e3

        dtb0 = 4 if pidx == 1 else 2
        def emit_dt_sp(k):
            dsl = slice((k % 2) * L, (k % 2 + 1) * L)
            bank = dtb0 + (k % 2)
            mm = self.Tm(lambda e, k=k, bank=bank: e.matmul(
                ps[bank][:], t.dtw[64:128, k * P:(k + 1) * P],
                t.dlt[64:128, :], start=True, stop=True),
                waits=[e1, self.tok_dtwload, self.bk_r[bank]], inc=True)
            self.bk_w[bank] = mm
            self.rd("dtw", mm)
            self.rd("dlt", mm)
            spA = t.cring[:, (k % 2) * L:(k % 2 + 1) * L]
            spB = t.sqring[:, (k % 2) * L:(k % 2 + 1) * L]
            dtb = self.col(t.smv, sc["dtb"] + boff * 16 + k)
            sA = self.S(lambda e, bank=bank, dtb=dtb, spA=spA: e.activation(
                out=spA, in_=ps[bank][:], func=AF.Abs, bias=dtb),
                waits=[mm] + self.war(("cring", k % 2)))
            self.wrote(("cring", k % 2), sA)
            sE = self.S(lambda e, spA=spA, spB=spB: e.activation(
                out=spB, in_=spA, func=AF.Exp,
                scale=self.col(t.smv, sc["cst"] + 3)),
                waits=self.war(("sqring", k % 2)))
            self.rd(("cring", k % 2), sE)
            self.wrote(("sqring", k % 2), sE)
            sLn = self.S(lambda e, spA=spA, spB=spB: e.activation(
                out=spA, in_=spB, func=AF.Ln,
                bias=self.col(t.smv, sc["cst"] + 1)))
            self.rd(("sqring", k % 2), sLn)
            self.wrote(("cring", k % 2), sLn)
            sR = self.S(lambda e, bank=bank, dtb=dtb, spB=spB: e.activation(
                out=spB, in_=ps[bank][:], func=AF.Relu, bias=dtb))
            self.bk_r[bank] = sR
            return (sR, spA, spB)

        def emit_vd(k, sp):
            sR, spA, spB = sp
            dsl = slice((k % 2) * L, (k % 2 + 1) * L)
            vd = self.V(lambda e, dsl=dsl, spA=spA, spB=spB: e.tensor_tensor(
                t.dring[:, dsl], spA, spB, Alu.add),
                waits=[sR] + self.war(("dr", k % 2)))
            self.wrote(("dr", k % 2), vd)
            self.rd(("cring", k % 2), vd)
            self.rd(("sqring", k % 2), vd)
            return vd

        sp_pend0 = emit_dt_sp(0)
        sp_pend1 = emit_dt_sp(1)
        vd0 = emit_vd(0, sp_pend0)

        idx = 0
        for n in range(NS):
            for i, (rows, dstt, key) in enumerate(
                    ((slice(0, 16), t.brep, "brep"),
                     (slice(32, 48), t.crep, "crep"))):
                bank = 2 + idx % 4
                mm = self.Tm(lambda e, n=n, rows=rows, bank=bank: e.matmul(
                    ps[bank][:], t.sel[rows, n * P:(n + 1) * P],
                    t.BC[rows, :], start=True, stop=True),
                    waits=[e3, self.bk_r[bank], self.tok_constg] +
                          (self.war(key) if n == 0 else []), inc=True)
                self.rd("BC", mm)
                self.bk_w[bank] = mm
                ev = self.S(lambda e, n=n, dstt=dstt, bank=bank:
                            e.activation(
                    out=dstt[:, n * L:(n + 1) * L], in_=ps[bank][:],
                    func=AF.Identity), waits=[mm])
                self.bk_r[bank] = ev
                if n == NS - 1:
                    self.wrote(key, ev)
                    tok_rep2 = ev
                idx += 1
        tok_reps = [tok_rep2, ev]

        self.oph_tok = [None] * 4
        if c + 1 < NCH:
            self.pre_gen = self.gen_pre(br, pidx, c + 1)
        vd_tok = [None] * KI
        sp_pend = [None] * KI
        sp_pend[0], sp_pend[1] = sp_pend0, sp_pend1
        vd_tok[0] = vd0
        op_last = [None] * 4

        def op_stream(kk, vtok):
            for m in range(4):
                lhs = t.W1[:, kk * 1024 + m * P:kk * 1024 + (m + 1) * P]
                w0 = [vtok] if m == 0 else []
                if kk == 0:
                    w0 = w0 + [self.tok_w1load, self.bk_r[m]]
                op_last[m] = self.Tm(
                    lambda e, lhs=lhs, kk=kk, m=m: e.matmul(
                        ps[m][:], lhs, t.ybf[:, kk * L:(kk + 1) * L],
                        start=(kk == 0), stop=(kk == KI - 1)),
                    waits=w0, inc=(kk == KI - 1))
                if kk == KI - 1:
                    self.bk_w[m] = op_last[m]
                    self.rd("W1", op_last[m])
                    self.rd("ybf", op_last[m])
        for k in range(KI):
            dsl = slice((k % 2) * L, (k % 2 + 1) * L)
            vd = vd_tok[k]
            vdu = self.V(lambda e, dsl=dsl, k=k: e.tensor_tensor(
                t.during[:, 0:L], t.dring[:, dsl],
                t.xcu[:, k * LP4 + 4:k * LP4 + 4 + L], Alu.mult),
                waits=[conv_tok[k], vd])
            self.rd(("xcu", k), vdu)
            ybank = 6 + (k % 2)
            ymm = None
            for g in range(NGRP):
                gslot = g % 2
                on_g = (g % 2 == 1)
                du_ap = t.during[:, 0:L].unsqueeze(1).broadcast_to([P, NGRP, L])
                buf = t.dbub[:, gslot * NGRP * L:(gslot + 1) * NGRP * L]
                def dbu_fn(e, du_ap=du_ap, buf=buf, g=g):
                    return e.tensor_tensor(
                        buf.rearrange("p (n l) -> p n l", n=NGRP), du_ap,
                        t.brep[:, g * NGRP * L:(g + 1) * NGRP * L].rearrange(
                            "p (n l) -> p n l", n=NGRP), Alu.mult)
                w0 = ([vdu] + tok_reps + self.war(("dbu", gslot)) +
                      [self.wt.get(("dbu", gslot))])
                gdb = self.V(dbu_fn, waits=w0)
                self.wrote(("dbu", gslot), gdb)
                self.rd("brep", gdb)
                for j in range(NGRP):
                    n = g * NGRP + j
                    esl = slice((n % 2) * L, (n % 2 + 1) * L)
                    se = self.S(lambda e, dsl=dsl, esl=esl, n=n, k=k:
                                e.activation(
                        out=t.ering[:, esl], in_=t.dring[:, dsl], func=AF.Exp,
                        scale=self.col(t.A_all, k * NS + n)),
                        waits=[vd, self.tok_A] + self.war(("er", n % 2)))
                    self.rd("A_all", se)
                    self.wrote(("er", n % 2), se)
                    self.rd(("dr", k % 2), se)
                    vs = self.V(lambda e, esl=esl, buf=buf, j=j, k=k,
                                n=n: e.tensor_tensor_scan(
                        t.hring[:, j * L:(j + 1) * L], t.ering[:, esl],
                        buf[:, j * L:(j + 1) * L],
                        self.col(t.state, k * NS + n), Alu.mult, Alu.add),
                        waits=[se, gdb, self.tok_statez])
                    self.rd(("er", n % 2), vs)
                vst = self.V(lambda e, k=k, g=g: e.tensor_copy(
                    t.state[:, k * NS + g * NGRP:k * NS + (g + 1) * NGRP],
                    t.hring[:, L - 1::L]))
                def hc_fn(e, buf=buf, g=g):
                    return e.tensor_tensor(
                        buf, t.hring[:],
                        t.crep[:, g * NGRP * L:(g + 1) * NGRP * L], Alu.mult)
                vhc = self.V(hc_fn, waits=tok_reps)
                self.rd("crep", vhc)
                self.wrote(("dbu", gslot), vhc)
                if g == 0 and k + 1 < KI:
                    if k >= 1:
                        sp_pend[k + 1] = emit_dt_sp(k + 1)
                    vd_tok[k + 1] = emit_vd(k + 1, sp_pend[k + 1])
                if g == 0:
                    vtok = self.emit_vy()
                    if pidx == 1 and k >= 1:
                        op_stream(k - 1, vtok)
                for j in range(NGRP):
                    w0 = ([vhc, self.bk_r[ybank]] if (g == 0 and j == 0)
                          else [vhc] if j == 0 else [])
                    ymm = self.Tm(lambda e, buf=buf, j=j, g=g, k=k,
                                  ybank=ybank: e.matmul(
                        ps[ybank][:], t.ident[:],
                        buf[:, j * L:(j + 1) * L],
                        start=(g == 0 and j == 0), stop=False),
                        waits=w0, inc=(j == NGRP - 1))
                    self.rd(("dbu", gslot), ymm)
            ymm = self.Tm(lambda e, k=k, ybank=ybank: e.matmul(
                ps[ybank][:], t.dD[:, k * P:(k + 1) * P],
                t.xcu[:, k * LP4 + 4:k * LP4 + 4 + L],
                start=False, stop=True),
                waits=[conv_tok[k], self.tok_dDload], inc=True)
            self.bk_w[ybank] = ymm
            self.rd("dD", ymm)
            self.rd(("xcu", k), ymm)
            self.vy_pend = (k, ybank, ymm)
            if k == KI - 1:
                tok_y = self.emit_vy()
                if pidx == 1:
                    op_stream(KI - 1, tok_y)

        

        if self.debug and pidx == 1:
            for k in range(KI):
                r1 = self.t_dbg1.rearrange("(k p) t -> p k t", p=P)
                r2 = self.t_dbg2.rearrange("(k p) t -> p k t", p=P)
                self.rd(("xcu", k), self.A(
                    r1[:, k:k + 1, c * L:(c + 1) * L],
                    t.xcu[:, k * LP4 + 4:k * LP4 + 4 + L].unsqueeze(1),
                    waits=[conv_tok[k]]))
                self.rd("ybf", self.A(
                    r2[:, k:k + 1, c * L:(c + 1) * L],
                    t.ybf[:, k * L:(k + 1) * L].unsqueeze(1),
                    waits=[tok_y]))
        if pidx == 2:
            hld = [None, None]
            for sl0 in (0, 1):
                w0 = self.war(("ldr", sl0)) + [self.store_tok.get(("h", c))]
                hld[sl0] = self.Y(ldr_dst(t.ldr, sl0),
                                  chunk_ap1(t.h_s, c, sl0), waits=w0)
                self.wrote(("ldr", sl0), hld[sl0])
            h2ld = [None, None]
            for sl0 in (0, 1):
                w0 = self.war(("uor", sl0)) + [self.store_tok.get(("h2", c))]
                h2ld[sl0] = self.G(uor_src1(t.uor, sl0),
                                   chunk_ap1(t.h2_s, c, sl0), waits=w0)
                self.wrote(("uor", sl0), h2ld[sl0])
        else:
            h2ld = [None, None]
            for sl0 in (0, 1):
                w0 = self.war(("ldr", sl0)) + [self.store_tok.get(("h2", c))]
                h2ld[sl0] = self.Y(ldr_dst(t.ldr, sl0),
                                   chunk_ap1(t.h2_s, c, sl0), waits=w0)
                self.wrote(("ldr", sl0), h2ld[sl0])
        hh_tok = [None] * KD
        for m in range(KD):
            bank = 2 + (m % 2)
            lastm = None
            if pidx == 1:
                if m < 4:
                    bank = m
                    lastm = op_last[m]
                else:
                    for k in range(KI):
                        lhs = t.W1[:, k * 1024 + m * P:k * 1024 + (m + 1) * P]
                        w0 = ([tok_y, self.tok_w1load, self.bk_r[bank]]
                              if k == 0 else [])
                        lastm = self.Tm(lambda e, lhs=lhs, k=k, bank=bank:
                                        e.matmul(
                            ps[bank][:], lhs, t.ybf[:, k * L:(k + 1) * L],
                            start=(k == 0), stop=(k == KI - 1)),
                            waits=w0, inc=(k == KI - 1))
                    self.rd("W1", lastm)
            else:
                slot = m % 2
                for k in range(KI):
                    lhs = t.R[:, slot * 2048 + k * P:slot * 2048 + (k + 1) * P]
                    w0 = ([tok_y, self.rq[slot], self.bk_r[bank]]
                          if k == 0 else [])
                    lastm = self.Tm(lambda e, lhs=lhs, k=k, bank=bank:
                                    e.matmul(
                        ps[bank][:], lhs, t.ybf[:, k * L:(k + 1) * L],
                        start=(k == 0), stop=(k == KI - 1)),
                        waits=w0, inc=(k == KI - 1))
                self.rd(("R", slot), lastm)
                self.rnext(slot)
            self.bk_w[bank] = lastm
            self.rd("ybf", lastm)
            hslot = m % 2
            tsl = slice(0, L)
            if pidx == 1:
                uslot = m % 2
                off = uslot * L
                ev = self.S(lambda e, bank=bank, tsl=tsl, m=m: e.activation(
                    out=t.tmpr[:, tsl], in_=ps[bank][:],
                    func=AF.Identity,
                    bias=self.col(t.smv, sc["outb"] + boff * 8 + m)),
                    waits=[lastm] + self.war(("tm", 0)))
                self.wrote(("tm", 0), ev)
                self.bk_r[bank] = ev
                vadd = self.V(lambda e, tsl=tsl, off=off, hslot=hslot:
                              e.tensor_tensor(
                    t.uor[:, off:off + L], t.tmpr[:, tsl],
                    t.ldr[:, hslot * L:(hslot + 1) * L], Alu.add),
                    waits=[ev, h2ld[hslot]] + self.war(("uor", uslot)))
                self.rd(("tm", 0), vadd)
                self.rd(("ldr", hslot), vadd)
                self.wrote(("uor", uslot), vadd)
                if m < KD - 2:
                    w0 = (self.war(("ldr", hslot)) +
                          [self.store_tok.get(("h2", c))])
                    h2ld[hslot] = self.Y(ldr_dst(t.ldr, hslot),
                                         chunk_ap1(t.h2_s, c, m + 2),
                                         waits=w0)
                    self.wrote(("ldr", hslot), h2ld[hslot])
                stt = self.A(chunk_ap1(t.u_s, c, m),
                             uor_src1(t.uor, uslot), waits=[vadd])
                self.rd(("uor", uslot), stt)
                self.store_tok[("u", c)] = stt
            else:
                ev = self.S(lambda e, bank=bank, tsl=tsl, m=m: e.activation(
                    out=t.tmpr[:, tsl], in_=ps[bank][:], func=AF.Identity,
                    bias=self.col(t.smv, sc["outb"] + boff * 8 + m)),
                    waits=[lastm] + self.war(("tm", 0)))
                self.wrote(("tm", 0), ev)
                self.bk_r[bank] = ev
                v1 = self.V(lambda e, tsl=tsl, hslot=hslot: e.tensor_tensor(
                    t.tmpr[:, tsl], t.tmpr[:, tsl],
                    t.uor[:, hslot * L:(hslot + 1) * L], Alu.add),
                    waits=[ev, h2ld[hslot]])
                self.rd(("tm", 0), v1)
                self.rd(("uor", hslot), v1)
                if m < KD - 2:
                    w0 = (self.war(("uor", hslot)) +
                          [self.store_tok.get(("h2", c))])
                    h2ld[hslot] = self.G(uor_src1(t.uor, hslot),
                                         chunk_ap1(t.h2_s, c, m + 2),
                                         waits=w0)
                    self.wrote(("uor", hslot), h2ld[hslot])
                ldx_m = self.pick(ld_x, m)
                v2 = self.V(lambda e, tsl=tsl, m=m: e.tensor_tensor(
                    t.fA[:, m * L:(m + 1) * L], t.tmpr[:, tsl],
                    t.fA[:, m * L:(m + 1) * L], Alu.mult),
                    waits=[ldx_m])
                self.rd(("tm", 0), v2)
                v3 = self.V(lambda e, m=m, hslot=hslot: e.tensor_tensor(
                    t.fA[:, m * L:(m + 1) * L], t.fA[:, m * L:(m + 1) * L],
                    t.ldr[:, hslot * L:(hslot + 1) * L], Alu.add),
                    waits=[hld[hslot]])
                self.rd(("ldr", hslot), v3)
                hh_tok[m] = v3
                if m < KD - 2:
                    nslot = m % 2
                    w0 = self.war(("ldr", nslot)) +                         [self.store_tok.get(("h", c))]
                    hld[nslot] = self.Y(ldr_dst(t.ldr, nslot),
                                        chunk_ap1(t.h_s, c, m + 2), waits=w0)
                    self.wrote(("ldr", nslot), hld[nslot])
        if pidx == 1:
            self.drain_pre(None)
            return
        self.wrote("fA", hh_tok[-1])
        if self.debug:
            sthh = self.A(chunk_ap(t.hh_s, c), chunk_dst(t.fA),
                          waits=[hh_tok[-1]])
            self.rd("fA", sthh)
        self.emit_ffn_tail(br, c, hh_tok)
        self.drain_pre(None)

    def emit_ffn_tail(self, br, c, hh_tok):
        t, ps = self.t, self.t.ps
        sc = SMV
        vrb, vmrb = self.ln_stats(t.fA, hh_tok[-1], True, "fA")
        tok_h3 = None
        for k in range(KD):
            sl = slice(k * L, (k + 1) * L)
            sps = slice(0, L)
            v1 = self.V(lambda e, sl=sl, sps=sps: e.tensor_tensor(
                t.spr[:, sps], t.fA[:, sl], t.rb[:], Alu.mult),
                waits=[vrb, hh_tok[k]] + self.war(("sp", 0)))
            self.rd("fA", v1)
            v2 = self.V(lambda e, sps=sps: e.tensor_tensor(
                t.spr[:, sps], t.spr[:, sps], t.mrb[:], Alu.subtract),
                waits=[vmrb])
            self.wrote(("sp", 0), v2)
            tok_h3 = self.S(lambda e, sl=sl, sps=sps, k=k: e.activation(
                out=t.ybf[:, sl], in_=t.spr[:, sps], func=AF.Identity,
                scale=self.col(t.smv, sc["gam"] + k),
                bias=self.col(t.smv, sc["bet"] + k)),
                waits=[v2] + (self.war("ybf") if k == 0 else []))
            self.rd(("sp", 0), tok_h3)
            self.rd("rb", v2)
            self.rd("mrb", v2)
        self.wrote("ybf", tok_h3)

        gtok = None
        for m in range(KF):
            pr = m // 2
            slot = pr % 2
            bank = 2 + (m % 2)
            lastm = None
            for k in range(KD):
                lhs = t.R[:, slot * 2048 + (m % 2) * 1024 + k * P:
                          slot * 2048 + (m % 2) * 1024 + (k + 1) * P]
                w0 = ([self.rq[slot], tok_h3, self.bk_r[bank]]
                      if k == 0 else [])
                lastm = self.Tm(lambda e, lhs=lhs, k=k, bank=bank: e.matmul(
                    ps[bank][:], lhs, t.ybf[:, k * L:(k + 1) * L],
                    start=(k == 0), stop=(k == KD - 1)),
                    waits=w0, inc=(k == KD - 1))
            self.bk_w[bank] = lastm
            self.rd(("R", slot), lastm)
            self.rd("ybf", lastm)
            if m % 2 == 1:
                self.rnext(slot)
            dstt = t.brep if m < 16 else t.crep
            moff = (m if m < 16 else m - 16) * L
            key = "brep" if m < 16 else "crep"
            ev = self.S(lambda e, bank=bank, dstt=dstt, moff=moff, m=m:
                        e.activation(
                out=dstt[:, moff:moff + L], in_=ps[bank][:], func=AF.Gelu,
                bias=self.col(t.smv, sc["fb1"] + m)),
                waits=[lastm] + (self.war(key) if (m % 16) == 0 else []))
            self.bk_r[bank] = ev
            if m == 15:
                self.wrote("brep", ev)
            if m == KF - 1:
                self.wrote("crep", ev)
            gtok = ev

        xld = [None, None]
        for sl0 in (0, 1):
            xld[sl0] = self.Y(ldr_dst(t.ldr, sl0), chunk_ap1(t.x_t, c, sl0),
                              waits=self.war(("ldr", sl0)))
            self.wrote(("ldr", sl0), xld[sl0])
        v2 = None
        for m in range(KD):
            slot = m % 2
            bank = 4 + (m % 2)
            lastm = None
            for k in range(KF):
                lhs = t.W1[:, KD * 1024 + slot * 4096 + k * P:
                           KD * 1024 + slot * 4096 + (k + 1) * P]
                dstt = t.brep if k < 16 else t.crep
                koff = (k if k < 16 else k - 16) * L
                w0 = ([self.w2ld[slot], gtok, self.bk_r[bank]]
                      if k == 0 else [])
                lastm = self.Tm(lambda e, lhs=lhs, dstt=dstt, koff=koff, k=k,
                                bank=bank: e.matmul(
                    ps[bank][:], lhs, dstt[:, koff:koff + L],
                    start=(k == 0), stop=(k == KF - 1)),
                    waits=w0, inc=(k == KF - 1))
            hslot = m % 2
            self.rd(("W2", slot), lastm)
            self.rd("brep", lastm)
            self.rd("crep", lastm)
            self.bk_w[bank] = lastm
            nm = m + 1
            if nm < KD or c + 1 < NCH:
                self.w2ld[nm % 2] = self.w2_load(nm % KD, nm % 2)
            tsl = slice(0, L)
            ev = self.S(lambda e, bank=bank, tsl=tsl, m=m: e.activation(
                out=t.tmpr[:, tsl], in_=ps[bank][:], func=AF.Identity,
                bias=self.col(t.smv, sc["fb2"] + m)),
                waits=[lastm] + self.war(("tm", 0)))
            self.wrote(("tm", 0), ev)
            self.bk_r[bank] = ev
            v1 = self.V(lambda e, tsl=tsl, m=m: e.tensor_tensor(
                t.tmpr[:, tsl], t.tmpr[:, tsl],
                t.fA[:, m * L:(m + 1) * L], Alu.add),
                waits=[ev, hh_tok[m]])
            self.rd(("tm", 0), v1)
            self.rd("fA", v1)
            v2 = self.V(lambda e, tsl=tsl, m=m, hslot=hslot: e.tensor_tensor(
                t.fA[:, m * L:(m + 1) * L], t.tmpr[:, tsl],
                t.ldr[:, hslot * L:(hslot + 1) * L], Alu.add),
                waits=[xld[hslot]] + (self.war("fA") if m == 0 else []))
            self.rd(("tm", 0), v2)
            self.rd(("ldr", hslot), v2)
            if m < KD - 2:
                nslot = m % 2
                xld[nslot] = self.Y(ldr_dst(t.ldr, nslot),
                                    chunk_ap1(t.x_t, c, m + 2),
                                    waits=self.war(("ldr", nslot)))
                self.wrote(("ldr", nslot), xld[nslot])
        self.wrote("fA", v2)
        tok_x1 = v2

        vrb2, vmrb2 = self.ln_stats(t.fA, tok_x1, True, "fA")
        xn = None
        for k in range(KD):
            sl = slice(k * L, (k + 1) * L)
            sps = slice(0, L)
            v1 = self.V(lambda e, sl=sl, sps=sps: e.tensor_tensor(
                t.spr[:, sps], t.fA[:, sl], t.rb[:], Alu.mult),
                waits=[vrb2] + self.war(("sp", 0)))
            self.rd("fA", v1)
            self.wrote(("sp", 0), v1)
            xn = self.V(lambda e, sl=sl, sps=sps: e.tensor_tensor(
                t.crep[:, sl], t.spr[:, sps], t.mrb[:], Alu.subtract),
                waits=[vmrb2] + (self.war("crep") if k == 0 else []))
            self.rd(("sp", 0), xn)
            self.rd("rb", xn)
            self.rd("mrb", xn)
        self.wrote("crep", xn)

        for m in range(KD):
            bank = 6 + (m % 2)
            lastm = None
            for k in range(KD):
                lhs = t.W1[:, k * 1024 + m * P:k * 1024 + (m + 1) * P]
                w0 = ([xn, self.tok_cbload, self.bk_r[bank]]
                      if k == 0 else [])
                lastm = self.Tm(lambda e, lhs=lhs, k=k, bank=bank: e.matmul(
                    ps[bank][:], lhs, t.crep[:, k * L:(k + 1) * L],
                    start=(k == 0), stop=(k == KD - 1)),
                    waits=w0, inc=(k == KD - 1))
            self.bk_w[bank] = lastm
            self.rd("W1", lastm)
            self.rd("crep", lastm)
            tsl = slice(0, L)
            ev = self.S(lambda e, bank=bank, tsl=tsl, m=m: e.activation(
                out=t.tmpr[:, tsl], in_=ps[bank][:], func=AF.Identity,
                scale=self.col(t.smv, sc["scbs"] + m),
                bias=self.col(t.smv, sc["fbias"] + m)),
                waits=[lastm, self.tok_scb] + self.war(("tm", 0)))
            self.wrote(("tm", 0), ev)
            self.bk_r[bank] = ev
            uslot = m % 2
            off = uslot * L
            vadd = self.V(lambda e, tsl=tsl, m=m, off=off: e.tensor_tensor(
                t.uor[:, off:off + L], t.tmpr[:, tsl],
                t.fA[:, m * L:(m + 1) * L], Alu.add),
                waits=[ev] + self.war(("uor", uslot)))
            self.rd(("tm", 0), vadd)
            self.rd("fA", vadd)
            self.wrote(("uor", uslot), vadd)
            stt = self.A(chunk_ap1(t.out_t, c, m),
                         uor_src1(t.uor, uslot), waits=[vadd])
            self.rd(("uor", uslot), stt)
        if c + 1 < NCH:
            self.ld_x = self.Y(chunk_dst(t.fA), chunk_ap(t.u_s, c + 1),
                               waits=self.war("fA") +
                               [self.store_tok.get(("u", c + 1))])
            self.wrote("fA", self.ld_x)

    def emit_pass3(self):
        t = self.t
        w1war = self.war("W1")
        self.tok_cbload = self.wrote(
            "W1", self.G(t.W1[:, 0:KD * 1024], t.cbt, waits=w1war))
        ld_hh = self.Y(chunk_dst(t.fA), chunk_ap(t.hh_s, 0),
                       waits=self.war("fA") + [self.store_tok.get(("hh", 0))])
        self.wrote("fA", ld_hh)
        self.wbase = 2 * 32768
        self.rq = [None, None]
        for pr in (0, 1):
            self.rq[pr % 2] = self.pair_load(t.fw1, pr, pr % 2)
        self.w2ld = [None, None]
        self.w2ld[0] = self.A(
            t.W1[:, KD * 1024:KD * 1024 + 4096],
            t.wbf[:, 3 * 32768:3 * 32768 + 4096],
            waits=w1war + self.war(("W2", 0)) + self.precast_toks)
        self.wrote(("W2", 0), self.w2ld[0])
        for c in range(NCH):
            ld_hh = self.emit_p3_chunk(c, ld_hh)

    def w2_load(self, m, slot):
        t = self.t
        dst = t.W1[:, KD * 1024 + slot * 4096:KD * 1024 + (slot + 1) * 4096]
        tok = self.A(dst, t.wbf[:, 3 * 32768 + m * 4096:
                                3 * 32768 + (m + 1) * 4096],
                     waits=self.war(("W2", slot)) +
                     [self.precast_piece[12 + m // 2]])
        self.wrote(("W2", slot), tok)
        return tok

    def emit_p3_chunk(self, c, ld_hh):
        t, ps = self.t, self.t.ps
        sc = SMV
        vrb, vmrb = self.ln_stats(t.fA, ld_hh, True)
        tok_h3 = None
        for k in range(KD):
            sl = slice(k * L, (k + 1) * L)
            sps = slice(0, L)
            ldh_k = self.pick(ld_hh, k)
            v1 = self.V(lambda e, sl=sl, sps=sps: e.tensor_tensor(
                t.spr[:, sps], t.fA[:, sl], t.rb[:], Alu.mult),
                waits=[vrb, ldh_k] + self.war(("sp", 0)))
            self.rd("fA", v1)
            v2 = self.V(lambda e, sps=sps: e.tensor_tensor(
                t.spr[:, sps], t.spr[:, sps], t.mrb[:], Alu.subtract),
                waits=[vmrb])
            self.wrote(("sp", 0), v2)
            tok_h3 = self.S(lambda e, sl=sl, sps=sps, k=k: e.activation(
                out=t.scrA[:, sl], in_=t.spr[:, sps], func=AF.Identity,
                scale=self.col(t.smv, sc["gam"] + k),
                bias=self.col(t.smv, sc["bet"] + k)),
                waits=[v2] + (self.war("scrA") if k == 0 else []))
            self.rd(("sp", 0), tok_h3)
            self.rd("rb", v2)
            self.rd("mrb", v2)
        self.wrote("scrA", tok_h3)

        gtok = None
        for m in range(KF):
            pr = m // 2
            slot = pr % 2
            bank = 2 + (m % 2)
            lastm = None
            for k in range(KD):
                lhs = t.R[:, slot * 2048 + (m % 2) * 1024 + k * P:
                          slot * 2048 + (m % 2) * 1024 + (k + 1) * P]
                w0 = ([self.rq[slot], tok_h3, self.bk_r[bank]]
                      if k == 0 else [])
                lastm = self.Tm(lambda e, lhs=lhs, k=k, bank=bank: e.matmul(
                    ps[bank][:], lhs, t.scrA[:, k * L:(k + 1) * L],
                    start=(k == 0), stop=(k == KD - 1)),
                    waits=w0, inc=(k == KD - 1))
            self.bk_w[bank] = lastm
            self.rd(("R", slot), lastm)
            self.rd("scrA", lastm)
            if m % 2 == 1:
                np_ = pr + 2
                if np_ <= 15:
                    self.rq[slot] = self.pair_load(t.fw1, np_, slot)
                elif c + 1 < NCH:
                    self.rq[slot] = self.pair_load(t.fw1, np_ - 16, slot)
            dstt = t.xcu if m < 16 else t.brep
            moff = (m if m < 16 else m - 16) * L
            key = "xcu" if m < 16 else "brep"
            ev = self.S(lambda e, bank=bank, dstt=dstt, moff=moff, m=m:
                        e.activation(
                out=dstt[:, moff:moff + L], in_=ps[bank][:], func=AF.Gelu,
                bias=self.col(t.smv, sc["fb1"] + m)),
                waits=[lastm] + (self.war(key) if (m % 16) == 0 else []))
            self.bk_r[bank] = ev
            if m == 15:
                self.wrote("xcu", ev)
            if m == KF - 1:
                self.wrote("brep", ev)
            gtok = ev

        xld = [None, None]
        for sl0 in (0, 1):
            xld[sl0] = self.Y(ldr_dst(t.ldr, sl0), chunk_ap1(t.x_t, c, sl0),
                              waits=self.war(("ldr", sl0)))
            self.wrote(("ldr", sl0), xld[sl0])
        v2 = None
        for m in range(KD):
            slot = m % 2
            bank = 4 + (m % 2)
            lastm = None
            for k in range(KF):
                lhs = t.W1[:, KD * 1024 + slot * 4096 + k * P:
                           KD * 1024 + slot * 4096 + (k + 1) * P]
                dstt = t.brep if k < 16 else t.crep
                koff = (k if k < 16 else k - 16) * L
                w0 = ([self.w2ld[slot], gtok, self.bk_r[bank]]
                      if k == 0 else [])
                lastm = self.Tm(lambda e, lhs=lhs, dstt=dstt, koff=koff, k=k,
                                bank=bank: e.matmul(
                    ps[bank][:], lhs, dstt[:, koff:koff + L],
                    start=(k == 0), stop=(k == KF - 1)),
                    waits=w0, inc=(k == KF - 1))
            self.bk_w[bank] = lastm
            self.rd(("W2", slot), lastm)
            self.rd("xcu", lastm)
            self.rd("brep", lastm)
            nm = m + 1
            if nm < KD or c + 1 < NCH:
                self.w2ld[nm % 2] = self.w2_load(nm % KD, nm % 2)
            tsl = slice(0, L)
            ev = self.S(lambda e, bank=bank, tsl=tsl, m=m: e.activation(
                out=t.tmpr[:, tsl], in_=ps[bank][:], func=AF.Identity,
                bias=self.col(t.smv, sc["fb2"] + m)),
                waits=[lastm] + self.war(("tm", 0)))
            self.wrote(("tm", 0), ev)
            self.bk_r[bank] = ev
            v1 = self.V(lambda e, tsl=tsl, m=m: e.tensor_tensor(
                t.tmpr[:, tsl], t.tmpr[:, tsl], t.fA[:, m * L:(m + 1) * L],
                Alu.add), waits=[ev])
            self.rd("fA", v1)
            hslot = m % 2
            v2 = self.V(lambda e, tsl=tsl, m=m, hslot=hslot:
                        e.tensor_tensor(
                t.fC[:, m * L:(m + 1) * L], t.tmpr[:, tsl],
                t.ldr[:, hslot * L:(hslot + 1) * L],
                Alu.add),
                waits=[xld[hslot]] + (self.war("fC") if m == 0 else []))
            self.rd(("tm", 0), v2)
            self.rd(("ldr", hslot), v2)
            if m < KD - 2:
                nslot = m % 2
                xld[nslot] = self.Y(ldr_dst(t.ldr, nslot),
                                    chunk_ap1(t.x_t, c, m + 2),
                                    waits=self.war(("ldr", nslot)))
                self.wrote(("ldr", nslot), xld[nslot])
            if c + 1 < NCH and m == KD - 1:
                self.nx_ldhh = self.A(
                    chunk_dst(t.fA), chunk_ap(t.hh_s, c + 1),
                    waits=self.war("fA") +
                    [self.store_tok.get(("hh", c + 1))])
                self.wrote("fA", self.nx_ldhh)
        self.wrote("fC", v2)
        tok_x1 = v2

        vrb2, vmrb2 = self.ln_stats(t.fC, tok_x1, True, "fC")
        xn = None
        for k in range(KD):
            sl = slice(k * L, (k + 1) * L)
            sps = slice(0, L)
            v1 = self.V(lambda e, sl=sl, sps=sps: e.tensor_tensor(
                t.spr[:, sps], t.fC[:, sl], t.rb[:], Alu.mult),
                waits=[vrb2] + self.war(("sp", 0)))
            self.rd("fC", v1)
            self.wrote(("sp", 0), v1)
            xn = self.V(lambda e, sl=sl, sps=sps: e.tensor_tensor(
                t.crep[:, sl], t.spr[:, sps], t.mrb[:], Alu.subtract),
                waits=[vmrb2] + (self.war("crep") if k == 0 else []))
            self.rd(("sp", 0), xn)
            self.rd("rb", xn)
            self.rd("mrb", xn)
        self.wrote("crep", xn)

        for m in range(KD):
            bank = 6 + (m % 2)
            lastm = None
            for k in range(KD):
                lhs = t.W1[:, k * 1024 + m * P:k * 1024 + (m + 1) * P]
                w0 = ([xn, self.tok_cbload, self.bk_r[bank]]
                      if k == 0 else [])
                lastm = self.Tm(lambda e, lhs=lhs, k=k, bank=bank: e.matmul(
                    ps[bank][:], lhs, t.crep[:, k * L:(k + 1) * L],
                    start=(k == 0), stop=(k == KD - 1)),
                    waits=w0, inc=(k == KD - 1))
            self.bk_w[bank] = lastm
            self.rd("W1", lastm)
            self.rd("crep", lastm)
            tsl = slice(0, L)
            ev = self.S(lambda e, bank=bank, tsl=tsl, m=m: e.activation(
                out=t.tmpr[:, tsl], in_=ps[bank][:], func=AF.Identity,
                scale=self.col(t.smv, sc["scbs"] + m),
                bias=self.col(t.smv, sc["fbias"] + m)),
                waits=[lastm, self.tok_scb] + self.war(("tm", 0)))
            self.wrote(("tm", 0), ev)
            self.bk_r[bank] = ev
            uslot = m % 2
            off = uslot * L
            vadd = self.V(lambda e, tsl=tsl, m=m, off=off: e.tensor_tensor(
                t.uor[:, off:off + L], t.tmpr[:, tsl],
                t.fC[:, m * L:(m + 1) * L], Alu.add),
                waits=[ev] + self.war(("uor", uslot)))
            self.rd(("tm", 0), vadd)
            self.rd("fC", vadd)
            self.wrote(("uor", uslot), vadd)
            stt = self.A(chunk_ap1(t.out_t, c, m),
                         uor_src1(t.uor, uslot), waits=[vadd])
            self.rd(("uor", uslot), stt)

        if c + 1 < NCH:
            return self.nx_ldhh
        return None

    def emit_epilogue(self):
        waits = []
        for q in ("a", "g", "y"):
            for i in range(self.NSEM):
                n = self.cnt.get(f"{q}{i}", 0)
                if n:
                    waits.append((f"{q}{i}", n, 0))
        self._emit("sync", lambda e: e.nop(), waits=tuple(waits))


def chunk_ap(dram, c):
    r = dram.rearrange("(k p) t -> p k t", p=P)
    return r[:, :, c * L:(c + 1) * L]


def chunk_aph(dram, c, lo):
    r = dram.rearrange("(k p) t -> p k t", p=P)
    return r[:, lo:lo + 4, c * L:(c + 1) * L]


def chunk_dsth(sbt, lo):
    return sbt[:, lo * L:(lo + 4) * L].rearrange("p (k l) -> p k l", k=4)


def chunk_ap1(dram, c, m):
    r = dram.rearrange("(k p) t -> p k t", p=P)
    return r[:, m:m + 1, c * L:(c + 1) * L]


def chunk_dst(sbt):
    return sbt[:].rearrange("p (k l) -> p k l", k=KD)


def ldr_dst(ldr, slot):
    return ldr[:, slot * L:(slot + 1) * L].unsqueeze(1)


def uor_src1(uor, slot):
    return uor[:, slot * L:(slot + 1) * L].unsqueeze(1)





def _vcol(v):
    n = v.shape[0] // P
    return np.ascontiguousarray(v.reshape(n, P).T.astype(np.float32))


def _rrm(a):
    kt = a.shape[0] // P
    mt = a.shape[1] // P
    b = a.reshape(kt, P, mt, P).transpose(1, 2, 0, 3)
    return np.ascontiguousarray(b.reshape(P, mt * kt * P).astype(np.float32))


def _rr(a):
    nk = a.shape[0] // P
    return np.ascontiguousarray(
        a.reshape(nk, P, a.shape[1]).transpose(1, 0, 2).reshape(P, -1)
        .astype(np.float32))


def prep_inputs(inputs):
    ins = {}
    for br in ("m1", "m2"):
        g = lambda nm: np.asarray(inputs[f"L0_attn_{br}_{nm}"], np.float32)
        ins[f"{br}_inw"] = _rrm(g("in_w"))
        ins[f"{br}_outw"] = (_rr(g("out_w")) if br == "m1"
                             else _rrm(g("out_w")))
        xp = np.zeros((DI, P), np.float32)
        xw = g("xproj_w")
        xp[:, 0:16] = xw[:, DTR:DTR + NS]
        xp[:, 32:48] = xw[:, DTR + NS:DTR + 2 * NS]
        xp[:, 64:128] = xw[:, 0:DTR]
        ins[f"{br}_xprj"] = _rr(xp)
        ins[f"{br}_dtw"] = np.ascontiguousarray(g("dt_w"))
        dd = np.zeros((P, DI), np.float32)
        Dv = g("D")
        idx = np.arange(P)
        for k in range(KI):
            dd[idx, k * P + idx] = Dv[k * P:(k + 1) * P]
        ins[f"{br}_dD"] = dd
    ins["alog"] = np.ascontiguousarray(np.concatenate(
        [_rr(np.asarray(inputs[f"L0_attn_{b}_A_log"], np.float32))
         for b in ("m1", "m2")], axis=1))
    smv = np.zeros((P, 384), np.float32)

    def put(name, v):
        smv[:, SMV[name]:SMV[name] + v.shape[1]] = v
    cat = lambda nm: np.concatenate(
        [_vcol(np.asarray(inputs[f"L0_attn_{b}_{nm}"], np.float32))
         for b in ("m1", "m2")], axis=1)
    put("inb", cat("in_b"))
    put("convw", np.concatenate(
        [_rr(np.asarray(inputs[f"L0_attn_{b}_conv_w"], np.float32)[:, 0, :])
         for b in ("m1", "m2")], axis=1))
    put("convb", cat("conv_b"))
    put("dtb", cat("dt_b"))
    put("outb", cat("out_b"))
    put("rmsw", cat("rms_w"))
    put("gam", _vcol(np.asarray(inputs["L0_attn_ln_gamma"], np.float32)))
    put("bet", _vcol(np.asarray(inputs["L0_attn_ln_beta"], np.float32)))
    put("fb1", _vcol(np.asarray(inputs["L0_attn_ffn_b1"], np.float32)))
    put("fb2", _vcol(np.asarray(inputs["L0_attn_ffn_b2"], np.float32)))
    put("scb", _vcol(np.asarray(inputs["L0_ffn_SCB"], np.float32)))
    put("fbias", _vcol(np.asarray(inputs["L0_ffn_bias"], np.float32)))
    smv[:, SMV["cst"] + 0] = EPS
    smv[:, SMV["cst"] + 1] = 1.0
    smv[:, SMV["cst"] + 2] = -0.5
    smv[:, SMV["cst"] + 3] = -1.0
    smv[:, SMV["cst"] + 4] = 1.0 + EPS
    smv[:, SMV["cst"] + 5] = EPS * EPS
    ins["smalls"] = smv
    ins["fw1"] = _rrm(np.asarray(inputs["L0_attn_ffn_w1"], np.float32))
    ins["fw2"] = _rrm(np.asarray(inputs["L0_attn_ffn_w2"], np.float32))
    ins["cbt"] = _rr(np.ascontiguousarray(
        np.asarray(inputs["L0_ffn_CB"], np.float32).T))
    ins["ident"] = np.eye(P, dtype=ml_dtypes.bfloat16)
    ins["identf"] = np.eye(P, dtype=np.float32)
    selm = np.zeros((48, NS * P), ml_dtypes.bfloat16)
    for n in range(NS):
        selm[n, n * P:(n + 1) * P] = 1.0
        selm[32 + n, n * P:(n + 1) * P] = 1.0
    ins["sel"] = selm
    ins["onec"] = np.ones((P, 1), ml_dtypes.bfloat16)
    ins["oner"] = np.ones((1, P), np.float32)
    return ins


def kernel(**inputs):
    ins = prep_inputs(inputs)
    x = np.asarray(inputs["L0_attn_x"], np.float32)
    xts = [np.ascontiguousarray(x[i].T) for i in range(8)]
    bld = Bld()
    nc = bld.build()
    in_maps = [dict(ins, x_t=xts[i]) for i in range(8)]
    res = run_bass_kernel_spmd(nc, in_maps, core_ids=list(range(8)))
    out = np.stack([r["out_t"].T for r in res.results], axis=0)
    return np.ascontiguousarray(out.astype(np.float32))


# version 37
# speedup vs baseline: 1.0012x; 1.0012x over previous
import numpy as np
import ml_dtypes
import concourse.bass as bass
import concourse.mybir as mybir
from concourse.alu_op_type import AluOpType as Alu
from concourse.bass_utils import run_bass_kernel_spmd
from contextlib import ExitStack

F32 = mybir.dt.float32
BF16 = mybir.dt.bfloat16
AF = mybir.ActivationFunctionType

P = 128
D = 1024; KD = 8
DI = 2048; KI = 16
NS = 16
DTR = 64
DFF = 4096; KF = 32
T = 8192
L = 512; NCH = T // L
LP4 = L + 4
EPS = 1e-5
NGRP = 4


SMV = {}
_off = 0
for _name, _n in (("inb", 64), ("convw", 128), ("convb", 32), ("dtb", 32),
                  ("outb", 16), ("rmsw", 16), ("gam", 8), ("bet", 8),
                  ("fb1", 32), ("fb2", 8), ("scb", 8), ("fbias", 8),
                  ("scbs", 8), ("cst", 8)):
    SMV[_name] = _off
    _off += _n
assert _off <= 384


class Bld:
    SECTIONS = ("sync", "scalar", "vector", "tensor", "gpsimd")

    def __init__(self, passes=(1, 1, 1), debug=False):
        self.passes = passes
        self.debug = debug
        self.ops = {s: [] for s in self.SECTIONS}
        self.cnt = {}
        self.rt = {}
        self.wt = {}
        self.bk_w = [None] * 8
        self.bk_r = [None] * 8
        self.store_tok = {}
        self.qidx = {}
        self.rms_tok = {}

    def _emit(self, section, fn, waits=(), inc=None):
        w = tuple(t for t in waits if t is not None)
        tok = None
        if inc is not None:
            name, amt = inc
            self.cnt[name] = self.cnt.get(name, 0) + amt
            tok = (name, self.cnt[name], amt)
        self.ops[section].append((fn, w, tok))
        return tok

    def V(self, fn, waits=(), inc=True):
        return self._emit("vector", fn, waits, ("v", 1) if inc else None)

    def S(self, fn, waits=(), inc=True):
        return self._emit("scalar", fn, waits, ("s", 1) if inc else None)

    def Tm(self, fn, waits=(), inc=False):
        return self._emit("tensor", fn, waits, ("t", 1) if inc else None)

    NSEM = 4

    def _dma(self, section, qname, out, in_, waits):
        i = self.qidx.get(qname, 0)
        self.qidx[qname] = i + 1
        sem = f"{qname}{i % self.NSEM}"
        pred = self.cnt.get(sem, 0)
        w = list(waits)
        if pred:
            w.append((sem, pred, 0))
        return self._emit(section, lambda e: e.dma_start(out=out, in_=in_),
                          w, (sem, 16))

    def Y(self, out, in_, waits=()):
        return self._dma("sync", "y", out, in_, waits)

    def A(self, out, in_, waits=()):
        return self._dma("scalar", "a", out, in_, waits)

    def G(self, out, in_, waits=()):
        return self._dma("gpsimd", "g", out, in_, waits)

    def rd(self, key, tok):
        if tok is None:
            return tok
        d = self.rt.setdefault(key, {})
        cur = d.get(tok[0])
        if cur is None or cur[1] < tok[1]:
            d[tok[0]] = tok
        return tok

    def war(self, key):
        return list(self.rt.get(key, {}).values())

    def wrote(self, key, tok):
        self.rt[key] = {}
        self.wt[key] = tok
        return tok

    def col(self, tensor, c):
        return tensor[:, c:c + 1]

    def cnt_toks_q(self, q):
        toks = []
        for i in range(self.NSEM):
            n = self.cnt.get(f"{q}{i}", 0)
            if n:
                toks.append((f"{q}{i}", n, 0))
        return toks

    OPH_BANKS = (0, 1, 4, 5)

    def emit_vy(self):
        if getattr(self, "vy_pend", None) is None:
            return None
        k, ybank, ymm = self.vy_pend
        self.vy_pend = None
        t, ps = self.t, self.t.ps
        ysl = t.scrA[:, (k % KD) * L:(k % KD + 1) * L]
        sev = self.S(lambda e, ysl=ysl, ybank=ybank: e.activation(
            out=ysl, in_=ps[ybank][:], func=AF.Identity),
            waits=[ymm] + self.war("scrA"))
        self.wrote("scrA", sev)
        self.bk_r[ybank] = sev
        vy = self.V(lambda e, k=k, ysl=ysl: e.tensor_tensor(
            t.ybf[:, k * L:(k + 1) * L], ysl,
            t.ybf[:, k * L:(k + 1) * L], Alu.mult),
            waits=[sev, self.sz_tok[k]])
        self.rd("scrA", vy)
        return vy

    @staticmethod
    def pick(tok, k):
        if (isinstance(tok, tuple) and len(tok) == 2
                and isinstance(tok[0], tuple)):
            return tok[k // 4]
        return tok

    def build(self):
        nc = bass.Bass("TRN2", target_bir_lowering=False,
                       detect_race_conditions=False)
        self.nc = nc
        dt = nc.dram_tensor

        x_t = dt("x_t", [D, T], F32, kind="ExternalInput").ap()
        win = {}
        for br in ("m1", "m2"):
            win[br, "inw"] = dt(f"{br}_inw", [P, KD * 4096], F32, kind="ExternalInput").ap()
            win[br, "outw"] = dt(f"{br}_outw", [P, KI * 1024], F32, kind="ExternalInput").ap()
            win[br, "xprj"] = dt(f"{br}_xprj", [P, KI * 128], F32, kind="ExternalInput").ap()
            win[br, "dtw"] = dt(f"{br}_dtw", [DTR, DI], F32, kind="ExternalInput").ap()
            win[br, "dD"] = dt(f"{br}_dD", [P, DI], F32, kind="ExternalInput").ap()
        alog_in = dt("alog", [P, 512], F32, kind="ExternalInput").ap()
        smalls = dt("smalls", [P, 384], F32, kind="ExternalInput").ap()
        fw1 = dt("fw1", [P, KD * 4096], F32, kind="ExternalInput").ap()
        fw2 = dt("fw2", [P, KF * 1024], F32, kind="ExternalInput").ap()
        cbt = dt("cbt", [P, KD * 1024], F32, kind="ExternalInput").ap()
        ident_in = dt("ident", [P, P], BF16, kind="ExternalInput").ap()
        identf_in = dt("identf", [P, P], F32, kind="ExternalInput").ap()
        sel_in = dt("sel", [48, NS * P], BF16, kind="ExternalInput").ap()
        onec_in = dt("onec", [P, 1], BF16, kind="ExternalInput").ap()
        oner_in = dt("oner", [1, P], F32, kind="ExternalInput").ap()

        out_t = dt("out_t", [D, T], F32, kind="ExternalOutput").ap()
        wbf = dt("wbf", [P, 4 * 32768 + 16384], BF16, kind="Internal").ap()
        skind = "ExternalOutput" if self.debug else "Internal"
        if self.debug:
            self.t_dbg1 = dt("dbg1", [DI, T], BF16, kind="ExternalOutput").ap()
            self.t_dbg2 = dt("dbg2", [DI, T], BF16, kind="ExternalOutput").ap()
        h_s = dt("h_s", [D, T], F32, kind=skind).ap()
        h2_s = dt("h2_s", [D, T], F32, kind=skind).ap()
        u_s = dt("u_s", [D, T], F32, kind=skind).ap()
        hh_s = dt("hh_s", [D, T], F32, kind=skind).ap()
        rms_d = dt("rms_d", [1, T], F32, kind="Internal").ap()

        with ExitStack() as ctx:
            sb = lambda n, s, d: ctx.enter_context(nc.sbuf_tensor(n, s, d))
            fA = sb("fA", [P, KD * L], F32)
            fC = sb("fC", [P, KD * L], F32)
            xcu = sb("xcu", [P, KI * LP4], BF16)
            brep = sb("brep", [P, NS * L], BF16)
            crep = sb("crep", [P, NS * L], BF16)
            ybf = sb("ybf", [P, KI * L], BF16)
            scrA = sb("scrA", [P, KD * L], BF16)
            W1 = sb("W1", [P, KI * 1024], BF16)
            R = sb("R", [P, 2 * 2048], BF16)
            dtw = sb("dtw", [P, DI], BF16)
            xprj = sb("xprj", [P, KI * P], BF16)
            dD = sb("dD", [P, DI], BF16)
            A_all = sb("A_all", [P, 256], F32)
            smv = sb("smv", [P, 384], F32)
            ident = sb("ident_sb", [P, P], BF16)
            identf = sb("identf_sb", [P, P], F32)
            sel = sb("sel_sb", [48, NS * P], BF16)
            onec = sb("onec_sb", [P, 1], BF16)
            oner = sb("oner_sb", [1, P], F32)
            rows4 = sb("rows4", [1, L], F32)
            cring = sb("cring", [P, 2 * L], BF16)
            sqring = sb("sqring", [P, 2 * L], BF16)
            rb = sb("rb", [P, L], F32)
            mrb = sb("mrb", [P, L], F32)
            spr = sb("spr", [P, L], F32)
            dlt = sb("dlt", [P, L], BF16)
            BC = sb("BC", [48, L], BF16)
            dring = sb("dring", [P, 2 * L], BF16)
            during = sb("during", [P, L], BF16)
            ering = sb("ering", [P, 2 * L], F32)
            dbub = sb("dbub", [P, 2 * NGRP * L], BF16)
            hring = sb("hring", [P, NGRP * L], BF16)
            state = sb("state", [P, 256], BF16)
            hist0 = sb("hist0", [P, KI * 3], BF16)
            hist1 = sb("hist1", [P, KI * 3], BF16)
            tmpr = sb("tmpr", [P, L], F32)
            uor = sb("uor", [P, 2 * L], F32)
            ldr = sb("ldr", [P, 2 * L], F32)
            ps = [ctx.enter_context(nc.psum_tensor(f"ps{i}", [P, L], F32))
                  for i in range(8)]
            semnames = ["v", "s", "t"] + [f"{q}{i}" for q in ("y", "a", "g")
                                          for i in range(self.NSEM)]
            sems = {n: ctx.enter_context(nc.semaphore(f"sem_{n}"))
                    for n in semnames}

            st = dict(locals())
            self.t = type("NS_", (), st)
            self.t.x_t = x_t; self.t.win = win; self.t.wbf = wbf
            self.t.out_t = out_t
            self.t.h_s = h_s; self.t.h2_s = h2_s
            self.t.u_s = u_s; self.t.hh_s = hh_s
            self.t.rms_d = rms_d
            self.t.alog_in = alog_in; self.t.smalls = smalls
            self.t.fw1 = fw1; self.t.fw2 = fw2; self.t.cbt = cbt
            self.t.ident_in = ident_in; self.t.sel_in = sel_in
            self.t.onec_in = onec_in; self.t.oner_in = oner_in
            self.t.identf_in = identf_in

            block = ctx.enter_context(nc.Block())

            self.emit_prologue()
            if self.passes[0]:
                self.emit_branch_pass("m1", 1)
            if self.passes[1]:
                self.emit_branch_pass("m2", 2)
            self.emit_epilogue()

            for sec in self.SECTIONS:
                own = {"vector": "v", "scalar": "s", "tensor": "t"}.get(sec)
                oplist = self.ops[sec]

                def make(oplist=oplist, own=own):
                    def body(eng):
                        waited = {}
                        for fn, waits, tok in oplist:
                            for wt in waits:
                                sem, val = wt[0], wt[1]
                                if sem == own:
                                    continue
                                if waited.get(sem, -1) >= val:
                                    continue
                                eng.wait_ge(sems[sem], val)
                                waited[sem] = val
                            inst = fn(eng)
                            if tok is not None:
                                inst.then_inc(sems[tok[0]], tok[2])
                    return body
                getattr(block, sec)(make())
        return nc

    def emit_prologue(self):
        t = self.t
        y1 = self.Y(t.smv[:], t.smalls)
        y2 = y1
        y3 = self.Y(t.oner[:], t.oner_in)
        self.tok_consts = y3
        self.tok_identf = self.Y(t.identf[:], t.identf_in)
        self.ld_x0 = self.Y(chunk_dst(t.fA), chunk_ap(t.x_t, 0))
        self.G(t.ident[:], t.ident_in)
        self.G(t.sel[:], t.sel_in)
        self.tok_constg = self.G(t.onec[:], t.onec_in)
        srcs = [t.win["m1", "inw"], t.win["m2", "inw"], t.fw1, t.fw2,
                t.win["m2", "outw"]]
        self.precast_piece = {}
        for si, srct in enumerate(srcs):
            for piece in range(4 if si < 4 else 2):
                half = (si * 4 + piece) % 2
                stg = t.W1[:, half * 8192:(half + 1) * 8192]
                gld = self.G(stg, srct[:, piece * 8192:(piece + 1) * 8192],
                             waits=self.war(("W1h", half)))
                self.wrote(("W1h", half), gld)
                yst = self.Y(t.wbf[:, si * 32768 + piece * 8192:
                                   si * 32768 + (piece + 1) * 8192],
                             stg, waits=[gld])
                self.rd(("W1h", half), yst)
                self.rd("W1", yst)
                self.precast_piece[si * 4 + piece] = yst
        self.precast_toks = self.cnt_toks_q("y")
        sc = SMV
        self.tok_scb = self.V(lambda e: e.tensor_scalar(
            out=t.smv[:, sc["scbs"]:sc["scbs"] + 8],
            in0=t.smv[:, sc["scb"]:sc["scb"] + 8],
            scalar1=1.0 / 127.0, scalar2=None, op0=Alu.mult),
            waits=[y1, y2, y3])

    def ln_stats(self, src, src_tok, with_mean, src_key=None,
                 combined=False, save_row=None):
        t, ps = self.t, self.t.ps
        onef = t.smv[:, SMV["cst"] + 1:SMV["cst"] + 2]
        mm_tok = None
        mm1_tok = None
        rd0 = self.bk_r[0]
        rd1 = self.bk_r[1]
        for k in range(KD):
            sl = slice(k * L, (k + 1) * L)
            cs = slice((k % 2) * L, (k % 2 + 1) * L)
            stk = self.pick(src_tok, k)
            stok = self.S(lambda e, sl=sl, cs=cs: e.activation(
                out=t.sqring[:, cs], in_=src[:, sl], func=AF.Square),
                waits=[stk] + self.war(("sqring", k % 2)))
            self.wrote(("sqring", k % 2), stok)
            if src_key:
                self.rd(src_key, stok)
            if with_mean:
                w0 = ([stk, rd0, self.tok_consts] if k == 0 else [stk])
                mm1 = self.Tm(lambda e, sl=sl, k=k, onef=onef: e.matmul(
                    ps[0][0:1, :], onef, src[:, sl],
                    start=(k == 0), stop=(k == KD - 1)),
                    waits=w0, inc=True)
                if src_key:
                    self.rd(src_key, mm1)
                mm1_tok = mm1
            w0 = ([stok, rd1, self.tok_constg] if k == 0 else [stok])
            mm2 = self.Tm(lambda e, cs=cs, k=k: e.matmul(
                ps[1][0:1, :], t.onec[:], t.sqring[:, cs],
                start=(k == 0), stop=(k == KD - 1)),
                waits=w0, inc=True)
            self.rd(("sqring", k % 2), mm2)
            mm_tok = mm2
        self.bk_w[0] = mm1_tok
        self.bk_w[1] = mm_tok
        sl0 = t.spr[0:1, 0:L]
        sl1 = t.mrb[0:1, 0:L]
        scx = t.rb[0:1, 0:L]
        war_sr = self.war(("sp", 0))
        war_rbrow = self.war("rb")
        war_mrow = self.war("mrb")
        if with_mean:
            vmu = self.S(lambda e: e.activation(
                out=sl0, in_=ps[0][0:1, :], func=AF.Identity,
                scale=t.smv[0:1, SMV["cst"] + 6:SMV["cst"] + 7]),
                waits=[mm1_tok, self.tok_consts] + war_sr)
            self.rd(("sp", 0), vmu)
            self.bk_r[0] = vmu
        vms = self.S(lambda e: e.activation(
            out=sl1, in_=ps[1][0:1, :], func=AF.Identity,
            scale=t.smv[0:1, SMV["cst"] + 6:SMV["cst"] + 7]),
            waits=[mm_tok, self.tok_consts] + war_mrow)
        self.bk_r[1] = vms
        if with_mean:
            self.V(lambda e: e.tensor_tensor(scx, sl0, sl0, Alu.mult),
                   waits=[vmu] + war_rbrow)
            vvar = self.V(lambda e: e.tensor_tensor(sl1, sl1, scx,
                                                    Alu.subtract),
                          waits=[vms])
        else:
            vvar = vms
        sln = self.S(lambda e: e.activation(
            out=scx, in_=sl1, func=AF.Ln,
            bias=t.smv[0:1, SMV["cst"]:SMV["cst"] + 1]),
            waits=[vvar, self.tok_consts] + war_rbrow)
        sln2 = None
        if combined:
            sln2 = self.S(lambda e: e.activation(
                out=t.rows4[0:1, 0:L], in_=sl1, func=AF.Ln,
                scale=t.smv[0:1, SMV["cst"] + 4:SMV["cst"] + 5],
                bias=t.smv[0:1, SMV["cst"] + 5:SMV["cst"] + 6]),
                waits=self.war("rows4"))
        s2 = self.S(lambda e: e.activation(
            out=sl1, in_=scx, func=AF.Exp,
            scale=t.smv[0:1, SMV["cst"] + 2:SMV["cst"] + 3]))
        if save_row is not None:
            srt = self.A(t.rms_d[0:1, save_row * L:(save_row + 1) * L],
                         sl1, waits=[s2])
            self.rd("mrb", srt)
            self.rms_tok[save_row] = srt
        s3 = None
        if combined:
            vdiff = self.V(lambda e: e.tensor_tensor(
                t.rows4[0:1, 0:L], t.rows4[0:1, 0:L], scx,
                Alu.subtract), waits=[sln2, sln])
            self.wrote("rows4", vdiff)
            s3 = self.S(lambda e: e.activation(
                out=t.rows4[0:1, 0:L], in_=t.rows4[0:1, 0:L],
                func=AF.Exp,
                scale=t.smv[0:1, SMV["cst"] + 2:SMV["cst"] + 3]),
                waits=[vdiff])
            self.rd("rows4", s3)
        tok_mr = None
        if with_mean:
            tok_mr = self.V(lambda e: e.tensor_tensor(sl0, sl0, sl1,
                                                      Alu.mult), waits=[s2])
            self.rd(("sp", 0), tok_mr)
        b1 = self.Tm(lambda e: e.matmul(ps[0][:], t.oner[0:1, :],
                                        sl1, start=True, stop=True),
                     waits=[s2, self.bk_r[0], self.tok_consts], inc=True)
        self.rd("mrb", b1)
        self.bk_w[0] = b1
        vrb = self.S(lambda e: e.activation(
            out=t.rb[:], in_=ps[0][:], func=AF.Identity),
            waits=[b1, s2])
        self.wrote("rb", vrb)
        self.bk_r[0] = vrb
        vmrb = None
        if with_mean:
            b2 = self.Tm(lambda e: e.matmul(ps[1][:], t.oner[0:1, :],
                                            sl0, start=True, stop=True),
                         waits=[tok_mr, self.bk_r[1]], inc=True)
            self.rd(("sp", 0), b2)
            self.bk_w[1] = b2
            vmrb = self.S(lambda e: e.activation(
                out=t.mrb[:], in_=ps[1][:], func=AF.Identity),
                waits=[b2] + self.war("mrb"))
            self.wrote("mrb", vmrb)
            self.bk_r[1] = vmrb
        if combined:
            b3 = self.Tm(lambda e: e.matmul(ps[0][:], t.oner[0:1, :],
                                            t.rows4[0:1, 0:L],
                                            start=True, stop=True),
                         waits=[s3, vrb], inc=True)
            self.rd("rows4", b3)
            self.bk_w[0] = b3
            return vrb, vmrb, b3
        return vrb, vmrb

    def pair_load(self, wsrc, pr, slot):
        return self.pair_load_off(self.wbase + pr * 2048, slot)

    def pair_load_off(self, off, slot):
        t = self.t
        dst = t.R[:, slot * 2048:(slot + 1) * 2048]
        tok = self.Y(dst, t.wbf[:, off:off + 2048],
                     waits=self.war(("R", slot)) +
                     [self.precast_piece[off // 8192]])
        self.wrote(("R", slot), tok)
        return tok

    def riter_merged(self):
        for pr in range(16):
            yield 32768 + pr * 2048
        for c in range(NCH):
            if c + 1 < NCH:
                for pr in range(8):
                    yield 32768 + pr * 2048
            for m in range(KD):
                yield 131072 + m * 2048
            for pr in range(16):
                yield 65536 + pr * 2048
            if c + 1 < NCH:
                for pr in range(8, 16):
                    yield 32768 + pr * 2048

    def rnext(self, slot):
        off = next(self.riter, None)
        if off is None:
            return self.rq[slot]
        self.rq[slot] = self.pair_load_off(off, slot)
        return self.rq[slot]

    def emit_branch_pass(self, br, pidx):
        t = self.t
        self.merged = (pidx == 2)
        if pidx == 1:
            self.tok_w1load = self.wrote(
                "W1", self.G(t.W1[:], t.win[br, "outw"],
                             waits=self.war("W1")))
        else:
            w1war = self.war("W1")
            self.tok_cbload = self.wrote(
                "W1", self.G(t.W1[:, 0:KD * 1024], t.cbt, waits=w1war))
            self.w2ld = [None, None]
            self.w2ld[0] = self.A(
                t.W1[:, KD * 1024:KD * 1024 + 4096],
                t.wbf[:, 3 * 32768:3 * 32768 + 4096],
                waits=w1war + self.war(("W2", 0)) +
                [self.precast_piece[12]])
            self.wrote(("W2", 0), self.w2ld[0])
            self.w2_use = [None, None]
        self.tok_dtwload = self.wrote(
            "dtw", self.G(t.dtw[64:128, :], t.win[br, "dtw"],
                          waits=self.war("dtw")))
        self.tok_xprjload = self.wrote(
            "xprj", self.G(t.xprj[:], t.win[br, "xprj"],
                           waits=self.war("xprj")))
        self.tok_dDload = self.wrote(
            "dD", self.G(t.dD[:], t.win[br, "dD"], waits=self.war("dD")))
        boff = 0 if br == "m1" else 1
        self.wbase = boff * 32768
        ya = self.A(t.spr[:, 0:256], t.alog_in[:, boff * 256:(boff + 1) * 256],
                    waits=self.war(("sp", 0)) + [self.wt.get(("sp", 0))])
        sa_ = self.S(lambda e: e.activation(out=t.A_all[:], in_=t.spr[:, 0:256],
                                            func=AF.Exp),
                     waits=[ya] + self.war("A_all"))
        self.rd(("sp", 0), sa_)
        self.tok_A = self.wrote("A_all", self.V(
            lambda e: e.tensor_scalar(out=t.A_all[:], in0=t.A_all[:],
                                      scalar1=-1.0, scalar2=None,
                                      op0=Alu.mult), waits=[sa_]))
        self.tok_statez = self.wrote(
            "state", self.V(lambda e: e.memset(t.state[:], 0.0),
                            waits=self.war("state")))
        vz = self.V(lambda e: e.memset(t.hist0[:], 0.0),
                    waits=self.war("hist0"))
        self.wrote("hist0", vz)
        vz2 = self.V(lambda e: e.memset(t.hist1[:], 0.0),
                     waits=self.war("hist1"))
        self.wrote("hist1", vz2)
        self.tok_histz = vz2
        if pidx == 1:
            self.ld_x = self.ld_x0
            self.ld_h2 = None
        else:
            self.ld_x = self.Y(chunk_dst(t.fA), chunk_ap(t.u_s, 0),
                               waits=self.war("fA") +
                               [self.store_tok.get(("u", 0))])
            self.ld_h2 = self.A(chunk_dst(t.fC), chunk_ap(t.h2_s, 0),
                                waits=self.war("fC") +
                                [self.store_tok.get(("h2", 0))])
        self.wrote("fA", self.ld_x)
        if self.ld_h2:
            self.wrote("fC", self.ld_h2)
        self.rq = [None, None]
        if self.merged:
            self.riter = self.riter_merged()
            for sl in (0, 1):
                self.rnext(sl)
        else:
            for pr in (0, 1):
                self.rq[pr % 2] = self.pair_load(t.win[br, "inw"], pr, pr % 2)
        self.prestate = {}
        self.pre_gen = self.gen_pre(br, pidx, 0)
        self.drain_pre(None)
        for c in range(NCH):
            self.emit_chunk_main(br, pidx, c)

    def drain_pre(self, n=None):
        if self.pre_gen is None:
            return
        try:
            if n is None:
                while True:
                    next(self.pre_gen)
            else:
                for _ in range(n):
                    next(self.pre_gen)
        except StopIteration:
            self.pre_gen = None

    def gen_pre(self, br, pidx, c):
        t, ps = self.t, self.t.ps
        sc = SMV
        boff = 0 if br == "m1" else 1
        ld_x = self.ld_x
        ld_h2 = self.ld_h2

        if pidx == 1:
            vrb, vmrb, b3 = self.ln_stats(t.fA, ld_x, True, "fA",
                                          combined=True)
            yield
            tok_h = None
            for k in range(KD):
                sl = slice(k * L, (k + 1) * L)
                self.V(lambda e, sl=sl: e.tensor_tensor(
                    t.fA[:, sl], t.fA[:, sl], t.rb[:], Alu.mult),
                    waits=[vrb, ld_x])
                tok_h = self.V(lambda e, sl=sl: e.tensor_tensor(
                    t.fA[:, sl], t.fA[:, sl], t.mrb[:], Alu.subtract),
                    waits=[vmrb])
                if k == 3:
                    yield
            self.rd("rb", tok_h)
            self.rd("mrb", tok_h)
            self.wrote("fA", tok_h)
            st_h = self.A(chunk_ap(t.h_s, c), chunk_dst(t.fA),
                          waits=[tok_h])
            self.store_tok[("h", c)] = self.rd("fA", st_h)
            vrr = self.S(lambda e: e.activation(
                out=t.rb[:], in_=ps[0][:], func=AF.Identity),
                waits=[b3] + self.war("rb"))
            self.wrote("rb", vrr)
            self.bk_r[0] = vrr
            yield
            tok_h2 = None
            for k in range(KD):
                sl = slice(k * L, (k + 1) * L)
                sps = slice(0, L)
                v1 = self.V(lambda e, sl=sl, sps=sps: e.tensor_tensor(
                    t.spr[:, sps], t.fA[:, sl], t.rb[:], Alu.mult),
                    waits=[vrr] + self.war(("sp", 0)))
                self.rd("fA", v1)
                self.rd("rb", v1)
                self.wrote(("sp", 0), v1)
                tok_h2 = self.S(lambda e, sl=sl, sps=sps, k=k: e.activation(
                    out=t.fC[:, sl], in_=t.spr[:, sps], func=AF.Identity,
                    scale=self.col(t.smv, sc["gam"] + k),
                    bias=self.col(t.smv, sc["bet"] + k)),
                    waits=[v1] + self.war("fC"))
                self.rd(("sp", 0), tok_h2)
                if k == 3:
                    yield
            self.wrote("fC", tok_h2)
            st2 = self.A(chunk_ap(t.h2_s, c), chunk_dst(t.fC),
                         waits=[tok_h2])
            self.store_tok[("h2", c)] = self.rd("fC", st2)
            if c + 1 < NCH:
                self.ld_x = self.Y(chunk_dst(t.fA), chunk_ap(t.x_t, c + 1),
                                   waits=self.war("fA"))
                self.wrote("fA", self.ld_x)
            yield
        else:
            tok_h2 = ld_h2
            for _ in range(4):
                yield

        if pidx == 1:
            vrq, _ = self.ln_stats(t.fC, tok_h2, False, "fC", save_row=c)
        else:
            ldrow = self.G(t.rows4[0:1, 0:L],
                           t.rms_d[0:1, c * L:(c + 1) * L],
                           waits=self.war("rows4") + [self.rms_tok[c]])
            self.wrote("rows4", ldrow)
            b1 = self.Tm(lambda e: e.matmul(
                ps[0][:], t.oner[0:1, :],
                t.rows4[0:1, 0:L], start=True, stop=True),
                waits=[ldrow, self.bk_r[0], self.tok_consts],
                inc=True)
            self.rd("rows4", b1)
            self.bk_w[0] = b1
            vrq = self.S(lambda e: e.activation(
                out=t.rb[:], in_=ps[0][:], func=AF.Identity),
                waits=[b1] + self.war("rb"))
            self.wrote("rb", vrq)
            self.bk_r[0] = vrq
        yield
        tok_xr = None
        for k in range(KD):
            sl = slice(k * L, (k + 1) * L)
            th2 = self.pick(tok_h2, k)
            w = [vrq, th2] + (self.war("scrA") if k == 0 else [])
            tok_xr = self.V(lambda e, sl=sl, k=k: e.scalar_tensor_tensor(
                t.scrA[:, sl], t.fC[:, sl],
                self.col(t.smv, sc["rmsw"] + boff * 8 + k), t.rb[:],
                Alu.mult, Alu.mult), waits=w)
            self.rd("fC", tok_xr)
            self.rd("rb", tok_xr)
        self.wrote("scrA", tok_xr)
        yield

        hp = t.hist0 if c % 2 == 0 else t.hist1
        hn = t.hist1 if c % 2 == 0 else t.hist0
        xcu3 = t.xcu[:].rearrange("p (k l) -> p k l", k=KI)
        hist3p = hp[:].rearrange("p (k h) -> p k h", k=KI)
        hist3n = hn[:].rearrange("p (k h) -> p k h", k=KI)
        conv_tok = [None] * KI
        pend = None
        for m in range(17):
          if m < 16:
            pr = m // 2
            slot = pr % 2
            bank = 4 + (m % 2)
            last = None
            for k in range(KD):
                lhs = t.R[:, slot * 2048 + (m % 2) * 1024 + k * P:
                          slot * 2048 + (m % 2) * 1024 + (k + 1) * P]
                w0 = [self.rq[slot], tok_xr, self.bk_r[bank]] if k == 0 else []
                last = self.Tm(lambda e, lhs=lhs, k=k, bank=bank: e.matmul(
                    ps[bank][:], lhs, t.scrA[:, k * L:(k + 1) * L],
                    start=(k == 0), stop=(k == KD - 1)),
                    waits=w0, inc=(k == KD - 1))
            self.bk_w[bank] = last
            self.rd(("R", slot), last)
            self.rd("scrA", last)
            if m % 2 == 1:
                if self.merged:
                    self.rnext(slot)
                elif pr + 2 <= 15:
                    self.rq[slot] = self.pair_load(t.win[br, "inw"], pr + 2,
                                                   slot)
          if pend is None:
            pend = (m, bank, last)
            yield
            continue
          m_cur, bank_cur, last_cur = m, bank, last
          m, bank, last = pend
          pend = (m_cur, bank_cur, last_cur) if m_cur < 16 else None
          if True:
            dst = t.xcu[:, m * LP4 + 4:m * LP4 + 4 + L]
            ev = self.S(lambda e, dst=dst, bank=bank, m=m: e.activation(
                out=dst, in_=ps[bank][:], func=AF.Identity,
                bias=self.col(t.smv, sc["inb"] + boff * 32 + m)),
                waits=[last] + self.war(("xcu", m)))
            self.bk_r[bank] = ev
            k = m
            base = k * LP4
            self.V(lambda e, k=k: e.tensor_copy(
                hist3n[:, k:k + 1, :], xcu3[:, k:k + 1, LP4 - 3:LP4]),
                waits=[ev, self.tok_histz])
            vpad = self.V(lambda e, k=k: e.tensor_copy(
                xcu3[:, k:k + 1, 1:4], hist3p[:, k:k + 1, :]))
            a1 = t.cring[:, (k % 2) * L:(k % 2 + 1) * L]
            a2 = t.sqring[:, (k % 2) * L:(k % 2 + 1) * L]
            wv = self.war(("cring", k % 2)) + self.war(("sqring", k % 2))
            v0 = self.V(lambda e, base=base, k=k, a1=a1: e.tensor_scalar(
                out=a1, in0=t.xcu[:, base + 1:base + 1 + L],
                scalar1=self.col(t.smv, sc["convw"] + boff * 64 + k * 4),
                scalar2=None, op0=Alu.mult), waits=wv)
            self.wrote(("cring", k % 2), v0)
            v1 = self.V(lambda e, base=base, k=k, a1=a1, a2=a2:
                        e.scalar_tensor_tensor(
                a2, t.xcu[:, base + 2:base + 2 + L],
                self.col(t.smv, sc["convw"] + boff * 64 + k * 4 + 1),
                a1, Alu.mult, Alu.add))
            self.wrote(("sqring", k % 2), v1)
            self.V(lambda e, base=base, k=k, a1=a1, a2=a2:
                   e.scalar_tensor_tensor(
                a1, t.xcu[:, base + 3:base + 3 + L],
                self.col(t.smv, sc["convw"] + boff * 64 + k * 4 + 2),
                a2, Alu.mult, Alu.add))
            v6 = self.V(lambda e, base=base, k=k, a1=a1, a2=a2:
                        e.scalar_tensor_tensor(
                a2, t.xcu[:, base + 4:base + 4 + L],
                self.col(t.smv, sc["convw"] + boff * 64 + k * 4 + 3),
                a1, Alu.mult, Alu.add))
            self.rd(("cring", k % 2), v6)
            self.rd(("sqring", k % 2), v6)
            s1 = self.S(lambda e, base=base, k=k, a2=a2: e.activation(
                out=t.xcu[:, base + 4:base + 4 + L], in_=a2, func=AF.Silu,
                bias=self.col(t.smv, sc["convb"] + boff * 16 + k)),
                waits=[v6])
            self.rd(("cring", k % 2), s1)
            self.rd(("sqring", k % 2), s1)
            self.wrote(("xcu", m), s1)
            conv_tok[k] = s1
            if pend is not None:
                yield
        self.prestate[c] = dict(conv_tok=conv_tok, tok_xr=tok_xr)

    def emit_chunk_main(self, br, pidx, c):
        t, ps = self.t, self.t.ps
        sc = SMV
        boff = 0 if br == "m1" else 1
        ld_x = self.ld_x
        ld_h2 = self.ld_h2

        pre = self.prestate.pop(c)
        conv_tok = pre["conv_tok"]
        tok_xr = pre["tok_xr"]
        if pidx == 2 and c + 1 < NCH:
            self.ld_h2 = self.A(chunk_dst(t.fC), chunk_ap(t.h2_s, c + 1),
                                waits=self.war("fC") +
                                [self.store_tok.get(("h2", c + 1))])
            self.wrote("fC", self.ld_h2)

        sz_tok = [None] * KI
        self.sz_tok = sz_tok
        for k in range(KI):
            mz = 16 + k
            pz = mz // 2
            slot = pz % 2
            zbank = 4 + (k % 2)
            zlast = None
            for kk in range(KD):
                lhs = t.R[:, slot * 2048 + (mz % 2) * 1024 + kk * P:
                          slot * 2048 + (mz % 2) * 1024 + (kk + 1) * P]
                w0 = [self.rq[slot], tok_xr, self.bk_r[zbank]] if kk == 0 else []
                zlast = self.Tm(lambda e, lhs=lhs, kk=kk, zbank=zbank:
                                e.matmul(
                    ps[zbank][:], lhs, t.scrA[:, kk * L:(kk + 1) * L],
                    start=(kk == 0), stop=(kk == KD - 1)),
                    waits=w0, inc=(kk == KD - 1))
            self.bk_w[zbank] = zlast
            self.rd(("R", slot), zlast)
            self.rd("scrA", zlast)
            if mz % 2 == 1:
                if self.merged:
                    self.rnext(slot)
                else:
                    np_ = pz + 2
                    if np_ <= 15:
                        self.rq[slot] = self.pair_load(t.win[br, "inw"], np_,
                                                       slot)
                    elif c + 1 < NCH:
                        self.rq[slot] = self.pair_load(t.win[br, "inw"],
                                                       np_ - 16, slot)
            ev = self.S(lambda e, zbank=zbank, k=k, mz=mz: e.activation(
                out=t.ybf[:, k * L:(k + 1) * L], in_=ps[zbank][:],
                func=AF.Silu,
                bias=self.col(t.smv, sc["inb"] + boff * 32 + mz)),
                waits=[zlast] + (self.war("ybf") if k == 0 else []))
            self.bk_r[zbank] = ev
            sz_tok[k] = ev
        self.wrote("ybf", sz_tok[-1])

        last = None
        for k in range(KI):
            lhs = t.xprj[:, k * P:(k + 1) * P]
            rhs = t.xcu[:, k * LP4 + 4:k * LP4 + 4 + L]
            w0 = ([conv_tok[-1], self.tok_xprjload, self.bk_r[4]]
                  if k == 0 else [conv_tok[k]])
            last = self.Tm(lambda e, lhs=lhs, rhs=rhs, k=k: e.matmul(
                ps[4][:], lhs, rhs, start=(k == 0), stop=(k == KI - 1)),
                waits=w0, inc=(k == KI - 1))
        self.bk_w[4] = last
        self.rd("xprj", last)
        for k in range(KI):
            self.rd(("xcu", k), last)
        e1 = self.S(lambda e: e.activation(out=t.dlt[64:128, :],
                                           in_=ps[4][64:128, :],
                                           func=AF.Identity),
                    waits=[last] + self.war("dlt"))
        self.wrote("dlt", e1)
        e2 = self.S(lambda e: e.activation(out=t.BC[0:16, :],
                                           in_=ps[4][0:16, :],
                                           func=AF.Identity),
                    waits=self.war("BC"))
        e3 = self.S(lambda e: e.activation(out=t.BC[32:48, :],
                                           in_=ps[4][32:48, :],
                                           func=AF.Identity))
        self.wrote("BC", e3)
        self.bk_r[4] = e3

        dtb0 = 4 if pidx == 1 else 2
        def emit_dt_sp(k):
            dsl = slice((k % 2) * L, (k % 2 + 1) * L)
            bank = dtb0 + (k % 2)
            mm = self.Tm(lambda e, k=k, bank=bank: e.matmul(
                ps[bank][:], t.dtw[64:128, k * P:(k + 1) * P],
                t.dlt[64:128, :], start=True, stop=True),
                waits=[e1, self.tok_dtwload, self.bk_r[bank]], inc=True)
            self.bk_w[bank] = mm
            self.rd("dtw", mm)
            self.rd("dlt", mm)
            spA = t.cring[:, (k % 2) * L:(k % 2 + 1) * L]
            spB = t.sqring[:, (k % 2) * L:(k % 2 + 1) * L]
            dtb = self.col(t.smv, sc["dtb"] + boff * 16 + k)
            sA = self.S(lambda e, bank=bank, dtb=dtb, spA=spA: e.activation(
                out=spA, in_=ps[bank][:], func=AF.Abs, bias=dtb),
                waits=[mm] + self.war(("cring", k % 2)))
            self.wrote(("cring", k % 2), sA)
            sE = self.S(lambda e, spA=spA, spB=spB: e.activation(
                out=spB, in_=spA, func=AF.Exp,
                scale=self.col(t.smv, sc["cst"] + 3)),
                waits=self.war(("sqring", k % 2)))
            self.rd(("cring", k % 2), sE)
            self.wrote(("sqring", k % 2), sE)
            sLn = self.S(lambda e, spA=spA, spB=spB: e.activation(
                out=spA, in_=spB, func=AF.Ln,
                bias=self.col(t.smv, sc["cst"] + 1)))
            self.rd(("sqring", k % 2), sLn)
            self.wrote(("cring", k % 2), sLn)
            sR = self.S(lambda e, bank=bank, dtb=dtb, spB=spB: e.activation(
                out=spB, in_=ps[bank][:], func=AF.Relu, bias=dtb))
            self.bk_r[bank] = sR
            return (sR, spA, spB)

        def emit_vd(k, sp):
            sR, spA, spB = sp
            dsl = slice((k % 2) * L, (k % 2 + 1) * L)
            vd = self.V(lambda e, dsl=dsl, spA=spA, spB=spB: e.tensor_tensor(
                t.dring[:, dsl], spA, spB, Alu.add),
                waits=[sR] + self.war(("dr", k % 2)))
            self.wrote(("dr", k % 2), vd)
            self.rd(("cring", k % 2), vd)
            self.rd(("sqring", k % 2), vd)
            return vd

        sp_pend0 = emit_dt_sp(0)
        sp_pend1 = emit_dt_sp(1)
        vd0 = emit_vd(0, sp_pend0)

        idx = 0
        for n in range(NS):
            for i, (rows, dstt, key) in enumerate(
                    ((slice(0, 16), t.brep, "brep"),
                     (slice(32, 48), t.crep, "crep"))):
                bank = 2 + idx % 4
                mm = self.Tm(lambda e, n=n, rows=rows, bank=bank: e.matmul(
                    ps[bank][:], t.sel[rows, n * P:(n + 1) * P],
                    t.BC[rows, :], start=True, stop=True),
                    waits=[e3, self.bk_r[bank], self.tok_constg] +
                          (self.war(key) if n == 0 else []), inc=True)
                self.rd("BC", mm)
                self.bk_w[bank] = mm
                ev = self.S(lambda e, n=n, dstt=dstt, bank=bank:
                            e.activation(
                    out=dstt[:, n * L:(n + 1) * L], in_=ps[bank][:],
                    func=AF.Identity), waits=[mm])
                self.bk_r[bank] = ev
                if n == NS - 1:
                    self.wrote(key, ev)
                    tok_rep2 = ev
                idx += 1
        tok_reps = [tok_rep2, ev]

        self.oph_tok = [None] * 4
        if c + 1 < NCH:
            self.pre_gen = self.gen_pre(br, pidx, c + 1)
        vd_tok = [None] * KI
        sp_pend = [None] * KI
        sp_pend[0], sp_pend[1] = sp_pend0, sp_pend1
        vd_tok[0] = vd0
        op_last = [None] * 4

        def op_stream(kk, vtok):
            for m in range(4):
                lhs = t.W1[:, kk * 1024 + m * P:kk * 1024 + (m + 1) * P]
                w0 = [vtok] if m == 0 else []
                if kk == 0:
                    w0 = w0 + [self.tok_w1load, self.bk_r[m]]
                op_last[m] = self.Tm(
                    lambda e, lhs=lhs, kk=kk, m=m: e.matmul(
                        ps[m][:], lhs, t.ybf[:, kk * L:(kk + 1) * L],
                        start=(kk == 0), stop=(kk == KI - 1)),
                    waits=w0, inc=(kk == KI - 1))
                if kk == KI - 1:
                    self.bk_w[m] = op_last[m]
                    self.rd("W1", op_last[m])
                    self.rd("ybf", op_last[m])
        for k in range(KI):
            dsl = slice((k % 2) * L, (k % 2 + 1) * L)
            vd = vd_tok[k]
            vdu = self.V(lambda e, dsl=dsl, k=k: e.tensor_tensor(
                t.during[:, 0:L], t.dring[:, dsl],
                t.xcu[:, k * LP4 + 4:k * LP4 + 4 + L], Alu.mult),
                waits=[conv_tok[k], vd])
            self.rd(("xcu", k), vdu)
            ybank = 6 + (k % 2)
            ymm = None
            for g in range(NGRP):
                gslot = g % 2
                on_g = (g % 2 == 1)
                du_ap = t.during[:, 0:L].unsqueeze(1).broadcast_to([P, NGRP, L])
                buf = t.dbub[:, gslot * NGRP * L:(gslot + 1) * NGRP * L]
                def dbu_fn(e, du_ap=du_ap, buf=buf, g=g):
                    return e.tensor_tensor(
                        buf.rearrange("p (n l) -> p n l", n=NGRP), du_ap,
                        t.brep[:, g * NGRP * L:(g + 1) * NGRP * L].rearrange(
                            "p (n l) -> p n l", n=NGRP), Alu.mult)
                w0 = ([vdu] + tok_reps + self.war(("dbu", gslot)) +
                      [self.wt.get(("dbu", gslot))])
                gdb = self.V(dbu_fn, waits=w0)
                self.wrote(("dbu", gslot), gdb)
                self.rd("brep", gdb)
                for j in range(NGRP):
                    n = g * NGRP + j
                    esl = slice((n % 2) * L, (n % 2 + 1) * L)
                    se = self.S(lambda e, dsl=dsl, esl=esl, n=n, k=k:
                                e.activation(
                        out=t.ering[:, esl], in_=t.dring[:, dsl], func=AF.Exp,
                        scale=self.col(t.A_all, k * NS + n)),
                        waits=[vd, self.tok_A] + self.war(("er", n % 2)))
                    self.rd("A_all", se)
                    self.wrote(("er", n % 2), se)
                    self.rd(("dr", k % 2), se)
                    vs = self.V(lambda e, esl=esl, buf=buf, j=j, k=k,
                                n=n: e.tensor_tensor_scan(
                        t.hring[:, j * L:(j + 1) * L], t.ering[:, esl],
                        buf[:, j * L:(j + 1) * L],
                        self.col(t.state, k * NS + n), Alu.mult, Alu.add),
                        waits=[se, gdb, self.tok_statez])
                    self.rd(("er", n % 2), vs)
                vst = self.V(lambda e, k=k, g=g: e.tensor_copy(
                    t.state[:, k * NS + g * NGRP:k * NS + (g + 1) * NGRP],
                    t.hring[:, L - 1::L]))
                def hc_fn(e, buf=buf, g=g):
                    return e.tensor_tensor(
                        buf, t.hring[:],
                        t.crep[:, g * NGRP * L:(g + 1) * NGRP * L], Alu.mult)
                vhc = self.V(hc_fn, waits=tok_reps)
                self.rd("crep", vhc)
                self.wrote(("dbu", gslot), vhc)
                if g == 0 and k + 1 < KI:
                    if k >= 1:
                        sp_pend[k + 1] = emit_dt_sp(k + 1)
                    vd_tok[k + 1] = emit_vd(k + 1, sp_pend[k + 1])
                if g == 0:
                    vtok = self.emit_vy()
                    if pidx == 1 and k >= 1:
                        op_stream(k - 1, vtok)
                for j in range(NGRP):
                    w0 = ([vhc, self.bk_r[ybank]] if (g == 0 and j == 0)
                          else [vhc] if j == 0 else [])
                    ymm = self.Tm(lambda e, buf=buf, j=j, g=g, k=k,
                                  ybank=ybank: e.matmul(
                        ps[ybank][:], t.ident[:],
                        buf[:, j * L:(j + 1) * L],
                        start=(g == 0 and j == 0), stop=False),
                        waits=w0, inc=(j == NGRP - 1))
                    self.rd(("dbu", gslot), ymm)
            ymm = self.Tm(lambda e, k=k, ybank=ybank: e.matmul(
                ps[ybank][:], t.dD[:, k * P:(k + 1) * P],
                t.xcu[:, k * LP4 + 4:k * LP4 + 4 + L],
                start=False, stop=True),
                waits=[conv_tok[k], self.tok_dDload], inc=True)
            self.bk_w[ybank] = ymm
            self.rd("dD", ymm)
            self.rd(("xcu", k), ymm)
            self.vy_pend = (k, ybank, ymm)
            if k == KI - 1:
                tok_y = self.emit_vy()
                if pidx == 1:
                    op_stream(KI - 1, tok_y)
        if pidx == 2:
            self.drain_pre(None)
        

        if self.debug and pidx == 1:
            for k in range(KI):
                r1 = self.t_dbg1.rearrange("(k p) t -> p k t", p=P)
                r2 = self.t_dbg2.rearrange("(k p) t -> p k t", p=P)
                self.rd(("xcu", k), self.A(
                    r1[:, k:k + 1, c * L:(c + 1) * L],
                    t.xcu[:, k * LP4 + 4:k * LP4 + 4 + L].unsqueeze(1),
                    waits=[conv_tok[k]]))
                self.rd("ybf", self.A(
                    r2[:, k:k + 1, c * L:(c + 1) * L],
                    t.ybf[:, k * L:(k + 1) * L].unsqueeze(1),
                    waits=[tok_y]))
        if pidx == 2:
            hld = [None, None]
            for sl0 in (0, 1):
                w0 = self.war(("ldr", sl0)) + [self.store_tok.get(("h", c))]
                hld[sl0] = self.Y(ldr_dst(t.ldr, sl0),
                                  chunk_ap1(t.h_s, c, sl0), waits=w0)
                self.wrote(("ldr", sl0), hld[sl0])
            h2ld = [None, None]
            for sl0 in (0, 1):
                w0 = self.war(("uor", sl0)) + [self.store_tok.get(("h2", c))]
                h2ld[sl0] = self.G(uor_src1(t.uor, sl0),
                                   chunk_ap1(t.h2_s, c, sl0), waits=w0)
                self.wrote(("uor", sl0), h2ld[sl0])
        else:
            h2ld = [None, None]
            for sl0 in (0, 1):
                w0 = self.war(("ldr", sl0)) + [self.store_tok.get(("h2", c))]
                h2ld[sl0] = self.Y(ldr_dst(t.ldr, sl0),
                                   chunk_ap1(t.h2_s, c, sl0), waits=w0)
                self.wrote(("ldr", sl0), h2ld[sl0])
        hh_tok = [None] * KD
        for m in range(KD):
            bank = 2 + (m % 2)
            lastm = None
            if pidx == 1:
                if m < 4:
                    bank = m
                    lastm = op_last[m]
                else:
                    for k in range(KI):
                        lhs = t.W1[:, k * 1024 + m * P:k * 1024 + (m + 1) * P]
                        w0 = ([tok_y, self.tok_w1load, self.bk_r[bank]]
                              if k == 0 else [])
                        lastm = self.Tm(lambda e, lhs=lhs, k=k, bank=bank:
                                        e.matmul(
                            ps[bank][:], lhs, t.ybf[:, k * L:(k + 1) * L],
                            start=(k == 0), stop=(k == KI - 1)),
                            waits=w0, inc=(k == KI - 1))
                    self.rd("W1", lastm)
            else:
                slot = m % 2
                for k in range(KI):
                    lhs = t.R[:, slot * 2048 + k * P:slot * 2048 + (k + 1) * P]
                    w0 = ([tok_y, self.rq[slot], self.bk_r[bank]]
                          if k == 0 else [])
                    lastm = self.Tm(lambda e, lhs=lhs, k=k, bank=bank:
                                    e.matmul(
                        ps[bank][:], lhs, t.ybf[:, k * L:(k + 1) * L],
                        start=(k == 0), stop=(k == KI - 1)),
                        waits=w0, inc=(k == KI - 1))
                self.rd(("R", slot), lastm)
                self.rnext(slot)
            self.bk_w[bank] = lastm
            self.rd("ybf", lastm)
            hslot = m % 2
            tsl = slice(0, L)
            if pidx == 1:
                uslot = m % 2
                off = uslot * L
                ev = self.S(lambda e, bank=bank, tsl=tsl, m=m: e.activation(
                    out=t.tmpr[:, tsl], in_=ps[bank][:],
                    func=AF.Identity,
                    bias=self.col(t.smv, sc["outb"] + boff * 8 + m)),
                    waits=[lastm] + self.war(("tm", 0)))
                self.wrote(("tm", 0), ev)
                self.bk_r[bank] = ev
                vadd = self.V(lambda e, tsl=tsl, off=off, hslot=hslot:
                              e.tensor_tensor(
                    t.uor[:, off:off + L], t.tmpr[:, tsl],
                    t.ldr[:, hslot * L:(hslot + 1) * L], Alu.add),
                    waits=[ev, h2ld[hslot]] + self.war(("uor", uslot)))
                self.rd(("tm", 0), vadd)
                self.rd(("ldr", hslot), vadd)
                self.wrote(("uor", uslot), vadd)
                if m < KD - 2:
                    w0 = (self.war(("ldr", hslot)) +
                          [self.store_tok.get(("h2", c))])
                    h2ld[hslot] = self.Y(ldr_dst(t.ldr, hslot),
                                         chunk_ap1(t.h2_s, c, m + 2),
                                         waits=w0)
                    self.wrote(("ldr", hslot), h2ld[hslot])
                stt = self.A(chunk_ap1(t.u_s, c, m),
                             uor_src1(t.uor, uslot), waits=[vadd])
                self.rd(("uor", uslot), stt)
                self.store_tok[("u", c)] = stt
            else:
                ev = self.S(lambda e, bank=bank, tsl=tsl, m=m: e.activation(
                    out=t.tmpr[:, tsl], in_=ps[bank][:], func=AF.Identity,
                    bias=self.col(t.smv, sc["outb"] + boff * 8 + m)),
                    waits=[lastm] + self.war(("tm", 0)))
                self.wrote(("tm", 0), ev)
                self.bk_r[bank] = ev
                v1 = self.V(lambda e, tsl=tsl, hslot=hslot: e.tensor_tensor(
                    t.tmpr[:, tsl], t.tmpr[:, tsl],
                    t.uor[:, hslot * L:(hslot + 1) * L], Alu.add),
                    waits=[ev, h2ld[hslot]])
                self.rd(("tm", 0), v1)
                self.rd(("uor", hslot), v1)
                if m < KD - 2:
                    w0 = (self.war(("uor", hslot)) +
                          [self.store_tok.get(("h2", c))])
                    h2ld[hslot] = self.G(uor_src1(t.uor, hslot),
                                         chunk_ap1(t.h2_s, c, m + 2),
                                         waits=w0)
                    self.wrote(("uor", hslot), h2ld[hslot])
                ldx_m = self.pick(ld_x, m)
                v2 = self.V(lambda e, tsl=tsl, m=m: e.tensor_tensor(
                    t.fA[:, m * L:(m + 1) * L], t.tmpr[:, tsl],
                    t.fA[:, m * L:(m + 1) * L], Alu.mult),
                    waits=[ldx_m])
                self.rd(("tm", 0), v2)
                v3 = self.V(lambda e, m=m, hslot=hslot: e.tensor_tensor(
                    t.fA[:, m * L:(m + 1) * L], t.fA[:, m * L:(m + 1) * L],
                    t.ldr[:, hslot * L:(hslot + 1) * L], Alu.add),
                    waits=[hld[hslot]])
                self.rd(("ldr", hslot), v3)
                hh_tok[m] = v3
                if m < KD - 2:
                    nslot = m % 2
                    w0 = self.war(("ldr", nslot)) +                         [self.store_tok.get(("h", c))]
                    hld[nslot] = self.Y(ldr_dst(t.ldr, nslot),
                                        chunk_ap1(t.h_s, c, m + 2), waits=w0)
                    self.wrote(("ldr", nslot), hld[nslot])
        if pidx == 1:
            self.drain_pre(None)
            return
        self.wrote("fA", hh_tok[-1])
        if self.debug:
            sthh = self.A(chunk_ap(t.hh_s, c), chunk_dst(t.fA),
                          waits=[hh_tok[-1]])
            self.rd("fA", sthh)
        self.emit_ffn_tail(br, c, hh_tok)

    def emit_ffn_tail(self, br, c, hh_tok):
        t, ps = self.t, self.t.ps
        sc = SMV
        vrb, vmrb = self.ln_stats(t.fA, hh_tok[-1], True, "fA")
        tok_h3 = None
        for k in range(KD):
            sl = slice(k * L, (k + 1) * L)
            sps = slice(0, L)
            v1 = self.V(lambda e, sl=sl, sps=sps: e.tensor_tensor(
                t.spr[:, sps], t.fA[:, sl], t.rb[:], Alu.mult),
                waits=[vrb, hh_tok[k]] + self.war(("sp", 0)))
            self.rd("fA", v1)
            v2 = self.V(lambda e, sps=sps: e.tensor_tensor(
                t.spr[:, sps], t.spr[:, sps], t.mrb[:], Alu.subtract),
                waits=[vmrb])
            self.wrote(("sp", 0), v2)
            tok_h3 = self.S(lambda e, sl=sl, sps=sps, k=k: e.activation(
                out=t.ybf[:, sl], in_=t.spr[:, sps], func=AF.Identity,
                scale=self.col(t.smv, sc["gam"] + k),
                bias=self.col(t.smv, sc["bet"] + k)),
                waits=[v2] + (self.war("ybf") if k == 0 else []))
            self.rd(("sp", 0), tok_h3)
            self.rd("rb", v2)
            self.rd("mrb", v2)
        self.wrote("ybf", tok_h3)

        gtok = None
        for m in range(KF):
            pr = m // 2
            slot = pr % 2
            bank = 2 + (m % 2)
            lastm = None
            for k in range(KD):
                lhs = t.R[:, slot * 2048 + (m % 2) * 1024 + k * P:
                          slot * 2048 + (m % 2) * 1024 + (k + 1) * P]
                w0 = ([self.rq[slot], tok_h3, self.bk_r[bank]]
                      if k == 0 else [])
                lastm = self.Tm(lambda e, lhs=lhs, k=k, bank=bank: e.matmul(
                    ps[bank][:], lhs, t.ybf[:, k * L:(k + 1) * L],
                    start=(k == 0), stop=(k == KD - 1)),
                    waits=w0, inc=(k == KD - 1))
            self.bk_w[bank] = lastm
            self.rd(("R", slot), lastm)
            self.rd("ybf", lastm)
            if m % 2 == 1:
                self.rnext(slot)
            dstt = t.brep if m < 16 else t.crep
            moff = (m if m < 16 else m - 16) * L
            key = "brep" if m < 16 else "crep"
            ev = self.S(lambda e, bank=bank, dstt=dstt, moff=moff, m=m:
                        e.activation(
                out=dstt[:, moff:moff + L], in_=ps[bank][:], func=AF.Gelu,
                bias=self.col(t.smv, sc["fb1"] + m)),
                waits=[lastm] + (self.war(key) if (m % 16) == 0 else []))
            self.bk_r[bank] = ev
            if m == 15:
                self.wrote("brep", ev)
            if m == KF - 1:
                self.wrote("crep", ev)
            gtok = ev

        xld = [None, None]
        for sl0 in (0, 1):
            xld[sl0] = self.Y(ldr_dst(t.ldr, sl0), chunk_ap1(t.x_t, c, sl0),
                              waits=self.war(("ldr", sl0)))
            self.wrote(("ldr", sl0), xld[sl0])
        v2 = None
        for m in range(KD):
            slot = m % 2
            bank = 4 + (m % 2)
            lastm = None
            for k in range(KF):
                lhs = t.W1[:, KD * 1024 + slot * 4096 + k * P:
                           KD * 1024 + slot * 4096 + (k + 1) * P]
                dstt = t.brep if k < 16 else t.crep
                koff = (k if k < 16 else k - 16) * L
                w0 = ([self.w2ld[slot], gtok, self.bk_r[bank]]
                      if k == 0 else [])
                lastm = self.Tm(lambda e, lhs=lhs, dstt=dstt, koff=koff, k=k,
                                bank=bank: e.matmul(
                    ps[bank][:], lhs, dstt[:, koff:koff + L],
                    start=(k == 0), stop=(k == KF - 1)),
                    waits=w0, inc=(k == KF - 1))
            hslot = m % 2
            self.rd(("W2", slot), lastm)
            self.rd("brep", lastm)
            self.rd("crep", lastm)
            self.bk_w[bank] = lastm
            nm = m + 1
            if nm < KD or c + 1 < NCH:
                self.w2ld[nm % 2] = self.w2_load(nm % KD, nm % 2)
            tsl = slice(0, L)
            ev = self.S(lambda e, bank=bank, tsl=tsl, m=m: e.activation(
                out=t.tmpr[:, tsl], in_=ps[bank][:], func=AF.Identity,
                bias=self.col(t.smv, sc["fb2"] + m)),
                waits=[lastm] + self.war(("tm", 0)))
            self.wrote(("tm", 0), ev)
            self.bk_r[bank] = ev
            v1 = self.V(lambda e, tsl=tsl, m=m: e.tensor_tensor(
                t.tmpr[:, tsl], t.tmpr[:, tsl],
                t.fA[:, m * L:(m + 1) * L], Alu.add),
                waits=[ev, hh_tok[m]])
            self.rd(("tm", 0), v1)
            self.rd("fA", v1)
            v2 = self.V(lambda e, tsl=tsl, m=m, hslot=hslot: e.tensor_tensor(
                t.fA[:, m * L:(m + 1) * L], t.tmpr[:, tsl],
                t.ldr[:, hslot * L:(hslot + 1) * L], Alu.add),
                waits=[xld[hslot]] + (self.war("fA") if m == 0 else []))
            self.rd(("tm", 0), v2)
            self.rd(("ldr", hslot), v2)
            if m < KD - 2:
                nslot = m % 2
                xld[nslot] = self.Y(ldr_dst(t.ldr, nslot),
                                    chunk_ap1(t.x_t, c, m + 2),
                                    waits=self.war(("ldr", nslot)))
                self.wrote(("ldr", nslot), xld[nslot])
        self.wrote("fA", v2)
        tok_x1 = v2

        vrb2, vmrb2 = self.ln_stats(t.fA, tok_x1, True, "fA")
        xn = None
        for k in range(KD):
            sl = slice(k * L, (k + 1) * L)
            sps = slice(0, L)
            v1 = self.V(lambda e, sl=sl, sps=sps: e.tensor_tensor(
                t.spr[:, sps], t.fA[:, sl], t.rb[:], Alu.mult),
                waits=[vrb2] + self.war(("sp", 0)))
            self.rd("fA", v1)
            self.wrote(("sp", 0), v1)
            xn = self.V(lambda e, sl=sl, sps=sps: e.tensor_tensor(
                t.crep[:, sl], t.spr[:, sps], t.mrb[:], Alu.subtract),
                waits=[vmrb2] + (self.war("crep") if k == 0 else []))
            self.rd(("sp", 0), xn)
            self.rd("rb", xn)
            self.rd("mrb", xn)
        self.wrote("crep", xn)

        for m in range(KD):
            bank = 6 + (m % 2)
            lastm = None
            for k in range(KD):
                lhs = t.W1[:, k * 1024 + m * P:k * 1024 + (m + 1) * P]
                w0 = ([xn, self.tok_cbload, self.bk_r[bank]]
                      if k == 0 else [])
                lastm = self.Tm(lambda e, lhs=lhs, k=k, bank=bank: e.matmul(
                    ps[bank][:], lhs, t.crep[:, k * L:(k + 1) * L],
                    start=(k == 0), stop=(k == KD - 1)),
                    waits=w0, inc=(k == KD - 1))
            self.bk_w[bank] = lastm
            self.rd("W1", lastm)
            self.rd("crep", lastm)
            tsl = slice(0, L)
            ev = self.S(lambda e, bank=bank, tsl=tsl, m=m: e.activation(
                out=t.tmpr[:, tsl], in_=ps[bank][:], func=AF.Identity,
                scale=self.col(t.smv, sc["scbs"] + m),
                bias=self.col(t.smv, sc["fbias"] + m)),
                waits=[lastm, self.tok_scb] + self.war(("tm", 0)))
            self.wrote(("tm", 0), ev)
            self.bk_r[bank] = ev
            uslot = m % 2
            off = uslot * L
            vadd = self.V(lambda e, tsl=tsl, m=m, off=off: e.tensor_tensor(
                t.uor[:, off:off + L], t.tmpr[:, tsl],
                t.fA[:, m * L:(m + 1) * L], Alu.add),
                waits=[ev] + self.war(("uor", uslot)))
            self.rd(("tm", 0), vadd)
            self.rd("fA", vadd)
            self.wrote(("uor", uslot), vadd)
            stt = self.A(chunk_ap1(t.out_t, c, m),
                         uor_src1(t.uor, uslot), waits=[vadd])
            self.rd(("uor", uslot), stt)
        if c + 1 < NCH:
            self.ld_x = self.Y(chunk_dst(t.fA), chunk_ap(t.u_s, c + 1),
                               waits=self.war("fA") +
                               [self.store_tok.get(("u", c + 1))])
            self.wrote("fA", self.ld_x)

    def emit_pass3(self):
        t = self.t
        w1war = self.war("W1")
        self.tok_cbload = self.wrote(
            "W1", self.G(t.W1[:, 0:KD * 1024], t.cbt, waits=w1war))
        ld_hh = self.Y(chunk_dst(t.fA), chunk_ap(t.hh_s, 0),
                       waits=self.war("fA") + [self.store_tok.get(("hh", 0))])
        self.wrote("fA", ld_hh)
        self.wbase = 2 * 32768
        self.rq = [None, None]
        for pr in (0, 1):
            self.rq[pr % 2] = self.pair_load(t.fw1, pr, pr % 2)
        self.w2ld = [None, None]
        self.w2ld[0] = self.A(
            t.W1[:, KD * 1024:KD * 1024 + 4096],
            t.wbf[:, 3 * 32768:3 * 32768 + 4096],
            waits=w1war + self.war(("W2", 0)) + self.precast_toks)
        self.wrote(("W2", 0), self.w2ld[0])
        for c in range(NCH):
            ld_hh = self.emit_p3_chunk(c, ld_hh)

    def w2_load(self, m, slot):
        t = self.t
        dst = t.W1[:, KD * 1024 + slot * 4096:KD * 1024 + (slot + 1) * 4096]
        tok = self.A(dst, t.wbf[:, 3 * 32768 + m * 4096:
                                3 * 32768 + (m + 1) * 4096],
                     waits=self.war(("W2", slot)) +
                     [self.precast_piece[12 + m // 2]])
        self.wrote(("W2", slot), tok)
        return tok

    def emit_p3_chunk(self, c, ld_hh):
        t, ps = self.t, self.t.ps
        sc = SMV
        vrb, vmrb = self.ln_stats(t.fA, ld_hh, True)
        tok_h3 = None
        for k in range(KD):
            sl = slice(k * L, (k + 1) * L)
            sps = slice(0, L)
            ldh_k = self.pick(ld_hh, k)
            v1 = self.V(lambda e, sl=sl, sps=sps: e.tensor_tensor(
                t.spr[:, sps], t.fA[:, sl], t.rb[:], Alu.mult),
                waits=[vrb, ldh_k] + self.war(("sp", 0)))
            self.rd("fA", v1)
            v2 = self.V(lambda e, sps=sps: e.tensor_tensor(
                t.spr[:, sps], t.spr[:, sps], t.mrb[:], Alu.subtract),
                waits=[vmrb])
            self.wrote(("sp", 0), v2)
            tok_h3 = self.S(lambda e, sl=sl, sps=sps, k=k: e.activation(
                out=t.scrA[:, sl], in_=t.spr[:, sps], func=AF.Identity,
                scale=self.col(t.smv, sc["gam"] + k),
                bias=self.col(t.smv, sc["bet"] + k)),
                waits=[v2] + (self.war("scrA") if k == 0 else []))
            self.rd(("sp", 0), tok_h3)
            self.rd("rb", v2)
            self.rd("mrb", v2)
        self.wrote("scrA", tok_h3)

        gtok = None
        for m in range(KF):
            pr = m // 2
            slot = pr % 2
            bank = 2 + (m % 2)
            lastm = None
            for k in range(KD):
                lhs = t.R[:, slot * 2048 + (m % 2) * 1024 + k * P:
                          slot * 2048 + (m % 2) * 1024 + (k + 1) * P]
                w0 = ([self.rq[slot], tok_h3, self.bk_r[bank]]
                      if k == 0 else [])
                lastm = self.Tm(lambda e, lhs=lhs, k=k, bank=bank: e.matmul(
                    ps[bank][:], lhs, t.scrA[:, k * L:(k + 1) * L],
                    start=(k == 0), stop=(k == KD - 1)),
                    waits=w0, inc=(k == KD - 1))
            self.bk_w[bank] = lastm
            self.rd(("R", slot), lastm)
            self.rd("scrA", lastm)
            if m % 2 == 1:
                np_ = pr + 2
                if np_ <= 15:
                    self.rq[slot] = self.pair_load(t.fw1, np_, slot)
                elif c + 1 < NCH:
                    self.rq[slot] = self.pair_load(t.fw1, np_ - 16, slot)
            dstt = t.xcu if m < 16 else t.brep
            moff = (m if m < 16 else m - 16) * L
            key = "xcu" if m < 16 else "brep"
            ev = self.S(lambda e, bank=bank, dstt=dstt, moff=moff, m=m:
                        e.activation(
                out=dstt[:, moff:moff + L], in_=ps[bank][:], func=AF.Gelu,
                bias=self.col(t.smv, sc["fb1"] + m)),
                waits=[lastm] + (self.war(key) if (m % 16) == 0 else []))
            self.bk_r[bank] = ev
            if m == 15:
                self.wrote("xcu", ev)
            if m == KF - 1:
                self.wrote("brep", ev)
            gtok = ev

        xld = [None, None]
        for sl0 in (0, 1):
            xld[sl0] = self.Y(ldr_dst(t.ldr, sl0), chunk_ap1(t.x_t, c, sl0),
                              waits=self.war(("ldr", sl0)))
            self.wrote(("ldr", sl0), xld[sl0])
        v2 = None
        for m in range(KD):
            slot = m % 2
            bank = 4 + (m % 2)
            lastm = None
            for k in range(KF):
                lhs = t.W1[:, KD * 1024 + slot * 4096 + k * P:
                           KD * 1024 + slot * 4096 + (k + 1) * P]
                dstt = t.brep if k < 16 else t.crep
                koff = (k if k < 16 else k - 16) * L
                w0 = ([self.w2ld[slot], gtok, self.bk_r[bank]]
                      if k == 0 else [])
                lastm = self.Tm(lambda e, lhs=lhs, dstt=dstt, koff=koff, k=k,
                                bank=bank: e.matmul(
                    ps[bank][:], lhs, dstt[:, koff:koff + L],
                    start=(k == 0), stop=(k == KF - 1)),
                    waits=w0, inc=(k == KF - 1))
            self.bk_w[bank] = lastm
            self.rd(("W2", slot), lastm)
            self.rd("xcu", lastm)
            self.rd("brep", lastm)
            nm = m + 1
            if nm < KD or c + 1 < NCH:
                self.w2ld[nm % 2] = self.w2_load(nm % KD, nm % 2)
            tsl = slice(0, L)
            ev = self.S(lambda e, bank=bank, tsl=tsl, m=m: e.activation(
                out=t.tmpr[:, tsl], in_=ps[bank][:], func=AF.Identity,
                bias=self.col(t.smv, sc["fb2"] + m)),
                waits=[lastm] + self.war(("tm", 0)))
            self.wrote(("tm", 0), ev)
            self.bk_r[bank] = ev
            v1 = self.V(lambda e, tsl=tsl, m=m: e.tensor_tensor(
                t.tmpr[:, tsl], t.tmpr[:, tsl], t.fA[:, m * L:(m + 1) * L],
                Alu.add), waits=[ev])
            self.rd("fA", v1)
            hslot = m % 2
            v2 = self.V(lambda e, tsl=tsl, m=m, hslot=hslot:
                        e.tensor_tensor(
                t.fC[:, m * L:(m + 1) * L], t.tmpr[:, tsl],
                t.ldr[:, hslot * L:(hslot + 1) * L],
                Alu.add),
                waits=[xld[hslot]] + (self.war("fC") if m == 0 else []))
            self.rd(("tm", 0), v2)
            self.rd(("ldr", hslot), v2)
            if m < KD - 2:
                nslot = m % 2
                xld[nslot] = self.Y(ldr_dst(t.ldr, nslot),
                                    chunk_ap1(t.x_t, c, m + 2),
                                    waits=self.war(("ldr", nslot)))
                self.wrote(("ldr", nslot), xld[nslot])
            if c + 1 < NCH and m == KD - 1:
                self.nx_ldhh = self.A(
                    chunk_dst(t.fA), chunk_ap(t.hh_s, c + 1),
                    waits=self.war("fA") +
                    [self.store_tok.get(("hh", c + 1))])
                self.wrote("fA", self.nx_ldhh)
        self.wrote("fC", v2)
        tok_x1 = v2

        vrb2, vmrb2 = self.ln_stats(t.fC, tok_x1, True, "fC")
        xn = None
        for k in range(KD):
            sl = slice(k * L, (k + 1) * L)
            sps = slice(0, L)
            v1 = self.V(lambda e, sl=sl, sps=sps: e.tensor_tensor(
                t.spr[:, sps], t.fC[:, sl], t.rb[:], Alu.mult),
                waits=[vrb2] + self.war(("sp", 0)))
            self.rd("fC", v1)
            self.wrote(("sp", 0), v1)
            xn = self.V(lambda e, sl=sl, sps=sps: e.tensor_tensor(
                t.crep[:, sl], t.spr[:, sps], t.mrb[:], Alu.subtract),
                waits=[vmrb2] + (self.war("crep") if k == 0 else []))
            self.rd(("sp", 0), xn)
            self.rd("rb", xn)
            self.rd("mrb", xn)
        self.wrote("crep", xn)

        for m in range(KD):
            bank = 6 + (m % 2)
            lastm = None
            for k in range(KD):
                lhs = t.W1[:, k * 1024 + m * P:k * 1024 + (m + 1) * P]
                w0 = ([xn, self.tok_cbload, self.bk_r[bank]]
                      if k == 0 else [])
                lastm = self.Tm(lambda e, lhs=lhs, k=k, bank=bank: e.matmul(
                    ps[bank][:], lhs, t.crep[:, k * L:(k + 1) * L],
                    start=(k == 0), stop=(k == KD - 1)),
                    waits=w0, inc=(k == KD - 1))
            self.bk_w[bank] = lastm
            self.rd("W1", lastm)
            self.rd("crep", lastm)
            tsl = slice(0, L)
            ev = self.S(lambda e, bank=bank, tsl=tsl, m=m: e.activation(
                out=t.tmpr[:, tsl], in_=ps[bank][:], func=AF.Identity,
                scale=self.col(t.smv, sc["scbs"] + m),
                bias=self.col(t.smv, sc["fbias"] + m)),
                waits=[lastm, self.tok_scb] + self.war(("tm", 0)))
            self.wrote(("tm", 0), ev)
            self.bk_r[bank] = ev
            uslot = m % 2
            off = uslot * L
            vadd = self.V(lambda e, tsl=tsl, m=m, off=off: e.tensor_tensor(
                t.uor[:, off:off + L], t.tmpr[:, tsl],
                t.fC[:, m * L:(m + 1) * L], Alu.add),
                waits=[ev] + self.war(("uor", uslot)))
            self.rd(("tm", 0), vadd)
            self.rd("fC", vadd)
            self.wrote(("uor", uslot), vadd)
            stt = self.A(chunk_ap1(t.out_t, c, m),
                         uor_src1(t.uor, uslot), waits=[vadd])
            self.rd(("uor", uslot), stt)

        if c + 1 < NCH:
            return self.nx_ldhh
        return None

    def emit_epilogue(self):
        waits = []
        for q in ("a", "g", "y"):
            for i in range(self.NSEM):
                n = self.cnt.get(f"{q}{i}", 0)
                if n:
                    waits.append((f"{q}{i}", n, 0))
        self._emit("sync", lambda e: e.nop(), waits=tuple(waits))


def chunk_ap(dram, c):
    r = dram.rearrange("(k p) t -> p k t", p=P)
    return r[:, :, c * L:(c + 1) * L]


def chunk_aph(dram, c, lo):
    r = dram.rearrange("(k p) t -> p k t", p=P)
    return r[:, lo:lo + 4, c * L:(c + 1) * L]


def chunk_dsth(sbt, lo):
    return sbt[:, lo * L:(lo + 4) * L].rearrange("p (k l) -> p k l", k=4)


def chunk_ap1(dram, c, m):
    r = dram.rearrange("(k p) t -> p k t", p=P)
    return r[:, m:m + 1, c * L:(c + 1) * L]


def chunk_dst(sbt):
    return sbt[:].rearrange("p (k l) -> p k l", k=KD)


def ldr_dst(ldr, slot):
    return ldr[:, slot * L:(slot + 1) * L].unsqueeze(1)


def uor_src1(uor, slot):
    return uor[:, slot * L:(slot + 1) * L].unsqueeze(1)





def _vcol(v):
    n = v.shape[0] // P
    return np.ascontiguousarray(v.reshape(n, P).T.astype(np.float32))


def _rrm(a):
    kt = a.shape[0] // P
    mt = a.shape[1] // P
    b = a.reshape(kt, P, mt, P).transpose(1, 2, 0, 3)
    return np.ascontiguousarray(b.reshape(P, mt * kt * P).astype(np.float32))


def _rr(a):
    nk = a.shape[0] // P
    return np.ascontiguousarray(
        a.reshape(nk, P, a.shape[1]).transpose(1, 0, 2).reshape(P, -1)
        .astype(np.float32))


def prep_inputs(inputs):
    ins = {}
    for br in ("m1", "m2"):
        g = lambda nm: np.asarray(inputs[f"L0_attn_{br}_{nm}"], np.float32)
        ins[f"{br}_inw"] = _rrm(g("in_w"))
        ins[f"{br}_outw"] = (_rr(g("out_w")) if br == "m1"
                             else _rrm(g("out_w")))
        xp = np.zeros((DI, P), np.float32)
        xw = g("xproj_w")
        xp[:, 0:16] = xw[:, DTR:DTR + NS]
        xp[:, 32:48] = xw[:, DTR + NS:DTR + 2 * NS]
        xp[:, 64:128] = xw[:, 0:DTR]
        ins[f"{br}_xprj"] = _rr(xp)
        ins[f"{br}_dtw"] = np.ascontiguousarray(g("dt_w"))
        dd = np.zeros((P, DI), np.float32)
        Dv = g("D")
        idx = np.arange(P)
        for k in range(KI):
            dd[idx, k * P + idx] = Dv[k * P:(k + 1) * P]
        ins[f"{br}_dD"] = dd
    ins["alog"] = np.ascontiguousarray(np.concatenate(
        [_rr(np.asarray(inputs[f"L0_attn_{b}_A_log"], np.float32))
         for b in ("m1", "m2")], axis=1))
    smv = np.zeros((P, 384), np.float32)

    def put(name, v):
        smv[:, SMV[name]:SMV[name] + v.shape[1]] = v
    cat = lambda nm: np.concatenate(
        [_vcol(np.asarray(inputs[f"L0_attn_{b}_{nm}"], np.float32))
         for b in ("m1", "m2")], axis=1)
    put("inb", cat("in_b"))
    put("convw", np.concatenate(
        [_rr(np.asarray(inputs[f"L0_attn_{b}_conv_w"], np.float32)[:, 0, :])
         for b in ("m1", "m2")], axis=1))
    put("convb", cat("conv_b"))
    put("dtb", cat("dt_b"))
    put("outb", cat("out_b"))
    put("rmsw", cat("rms_w"))
    put("gam", _vcol(np.asarray(inputs["L0_attn_ln_gamma"], np.float32)))
    put("bet", _vcol(np.asarray(inputs["L0_attn_ln_beta"], np.float32)))
    put("fb1", _vcol(np.asarray(inputs["L0_attn_ffn_b1"], np.float32)))
    put("fb2", _vcol(np.asarray(inputs["L0_attn_ffn_b2"], np.float32)))
    put("scb", _vcol(np.asarray(inputs["L0_ffn_SCB"], np.float32)))
    put("fbias", _vcol(np.asarray(inputs["L0_ffn_bias"], np.float32)))
    smv[:, SMV["cst"] + 0] = EPS
    smv[:, SMV["cst"] + 1] = 1.0
    smv[:, SMV["cst"] + 2] = -0.5
    smv[:, SMV["cst"] + 3] = -1.0
    smv[:, SMV["cst"] + 4] = 1.0 + EPS
    smv[:, SMV["cst"] + 5] = EPS * EPS
    smv[:, SMV["cst"] + 6] = 1.0 / D
    ins["smalls"] = smv
    ins["fw1"] = _rrm(np.asarray(inputs["L0_attn_ffn_w1"], np.float32))
    ins["fw2"] = _rrm(np.asarray(inputs["L0_attn_ffn_w2"], np.float32))
    ins["cbt"] = _rr(np.ascontiguousarray(
        np.asarray(inputs["L0_ffn_CB"], np.float32).T))
    ins["ident"] = np.eye(P, dtype=ml_dtypes.bfloat16)
    ins["identf"] = np.eye(P, dtype=np.float32)
    selm = np.zeros((48, NS * P), ml_dtypes.bfloat16)
    for n in range(NS):
        selm[n, n * P:(n + 1) * P] = 1.0
        selm[32 + n, n * P:(n + 1) * P] = 1.0
    ins["sel"] = selm
    ins["onec"] = np.ones((P, 1), ml_dtypes.bfloat16)
    ins["oner"] = np.ones((1, P), np.float32)
    return ins


def kernel(**inputs):
    ins = prep_inputs(inputs)
    x = np.asarray(inputs["L0_attn_x"], np.float32)
    xts = [np.ascontiguousarray(x[i].T) for i in range(8)]
    bld = Bld()
    nc = bld.build()
    in_maps = [dict(ins, x_t=xts[i]) for i in range(8)]
    res = run_bass_kernel_spmd(nc, in_maps, core_ids=list(range(8)))
    out = np.stack([r["out_t"].T for r in res.results], axis=0)
    return np.ascontiguousarray(out.astype(np.float32))
